# version 03
# baseline (speedup 1.0000x reference)
import numpy as np
import concourse.bass as bass
import concourse.mybir as mybir
from concourse.bass_utils import run_bass_kernel_spmd

F32 = mybir.dt.float32
BF16 = mybir.dt.bfloat16
I32 = mybir.dt.int32
AF = mybir.ActivationFunctionType
OP = mybir.AluOpType
AX = mybir.AxisListType
IOffs = bass.IndirectOffsetOnAxis

B, L, D, N = 4, 4096, 1024, 16
DT_RANK = 64
E, FF, CAP = 8, 4096, 512
NSLOT = E * CAP
TPC = 2048
NTT = TPC // 128
NCH = D // 128
NLT = L // 512

_DEV_TAPS = ()
_LAST_PHASE = 10

_ENGINES = ("sync", "scalar", "vector", "tensor", "gpsimd")


class Ctx:

    def __init__(self, nc, sems):
        self.nc = nc
        self.sems = sems
        self.active = None
        self.eng = None
        self.cnt = {k: 0 for k in sems}
        self.queue_sem = {"sync": "dS", "scalar": "dA", "gpsimd": "dG"}

    def begin(self, engine, eng_handle):
        self.active = engine
        self.eng = eng_handle
        self.cnt = {k: 0 for k in self.sems}

    def dma(self, queue, out, in_, **kw):
        s = self.queue_sem[queue]
        self.cnt[s] += 16
        if self.active == queue:
            self.eng.dma_start(out=out, in_=in_, **kw).then_inc(self.sems[s], 16)
        return self.cnt[s]

    def idma(self, out, out_off, in_, in_off):
        self.cnt["dG"] += 16
        if self.active == "gpsimd":
            self.eng.indirect_dma_start(out=out, out_offset=out_off, in_=in_,
                                        in_offset=in_off).then_inc(self.sems["dG"], 16)
        return self.cnt["dG"]

    def op(self, engine, fn, inc=1):
        sem = {"scalar": "sP", "vector": "vP", "tensor": "tP", "gpsimd": "gP"}[engine]
        self.cnt[sem] += inc
        if self.active == engine:
            instr = fn(self.eng)
            if inc:
                if engine == "tensor":
                    self.eng.drain().then_inc(self.sems[sem], inc)
                else:
                    instr.then_inc(self.sems[sem], inc)
        return self.cnt[sem]

    def wait(self, engine, sem, val=None):
        v = self.cnt[sem] if val is None else val
        if self.active == engine and v > 0:
            self.eng.wait_ge(self.sems[sem], v)

    def barrier(self):
        for s in ("dS", "dA", "dG", "vP", "sP", "tP", "gP"):
            for e in _ENGINES:
                self.wait(e, s)
        self.cnt["bar"] += 5
        if self.active is not None:
            self.eng.sem_inc(self.sems["bar"], 1)
            self.eng.wait_ge(self.sems["bar"], self.cnt["bar"])


def _build():
    nc = bass.Bass("TRN2", target_bir_lowering=False)

    def din(name, shape, dt=F32):
        return nc.dram_tensor(name, shape, dt, kind="ExternalInput").ap()

    def dscratch(name, shape, dt=F32):
        kind = "ExternalOutput" if name in _DEV_TAPS else "Internal"
        return nc.dram_tensor(name, shape, dt, kind=kind).ap()

    x_in = din("x", [L, D])
    tokl = din("tok_l", [128, 16], I32)
    ckl = din("ck_l", [128, 24])
    alog = din("A_log", [D, N])
    wcatl = din("wcat_l", [128, 1024])
    bbcl = din("b_bc_l", [128, 1])
    dcol_in = din("Dco_l", [128, 8])


    wdtp = din("W_dtp", [DT_RANK, D])
    bdtpl = din("b_dtp_l", [128, 8])
    centl = din("cent_l", [128, 64])
    lng = din("ln_g", [E, D])
    lnb = din("ln_b", [E, D])
    w1 = din("w1", [E, D, FF])
    b1l = din("b1_l", [128, 256])
    w2 = din("w2", [E, FF, D])
    b2l = din("b2_l", [128, 64])
    c_ident = din("c_ident", [128, 128])
    c_ones = din("c_ones", [128, 8])
    c_iotaE = din("c_iotaE", [8, 1])
    c_onerow = din("c_onerow", [1, 4096])
    c_zrow = din("c_zrow", [1, 4096])
    c_eps = din("c_eps", [128, 1])

    out_rows = nc.dram_tensor("out_rows", [TPC, D], F32, kind="ExternalOutput").ap()

    x_fm = dscratch("x_fm", [D, L])
    lnsig_fm = dscratch("lnsig_fm", [D, L])
    resid_fm = dscratch("resid_fm", [D, L])
    xln1_fm = dscratch("xln1_fm", [D, L])
    xln1_tm = dscratch("xln1_tm", [L, D])
    negBC_d = dscratch("negBC_d", [2 * N, L])
    rows_d = dscratch("rows_d", [2, L])
    u_dbg = dscratch("u_dbg", [D, L])
    route_dbg = dscratch("route_dbg", [4, TPC])
    aff_dbg = dscratch("aff_dbg", [128, 128])
    oh_dbg = dscratch("oh_dbg", [128, 96])
    h_dbg = dscratch("h_dbg", [128, 16384])
    slotx = dscratch("slotx", [NSLOT + 128, D])
    sloty = dscratch("sloty", [NSLOT + 128, D])

    BIGA = nc.alloc_sbuf_tensor("BIGA", [128, 16384], F32).ap()
    BIGB = nc.alloc_sbuf_tensor("BIGB", [128, 8192], F32).ap()
    BIGC = nc.alloc_sbuf_tensor("BIGC", [128, 8192], F32).ap()
    D1 = nc.alloc_sbuf_tensor("D1", [128, 4096], F32).ap()
    D2 = nc.alloc_sbuf_tensor("D2", [128, 4096], F32).ap()
    D3 = nc.alloc_sbuf_tensor("D3", [128, 4096], F32).ap()
    ident = nc.alloc_sbuf_tensor("ident", [128, 128], F32).ap()
    aexp = nc.alloc_sbuf_tensor("aexp", [128, 128], F32).ap()
    ckt = nc.alloc_sbuf_tensor("ckt", [128, 24], F32).ap()
    dp1 = nc.alloc_sbuf_tensor("dp1", [128, 8], F32).ap()
    dcols = nc.alloc_sbuf_tensor("dcols", [128, 8], F32).ap()
    alogs = nc.alloc_sbuf_tensor("alogs", [128, 128], F32).ap()
    wcats = nc.alloc_sbuf_tensor("wcats", [128, 1024], F32).ap()
    wdtps = nc.alloc_sbuf_tensor("wdtps", [64, 1024], F32).ap()
    bbcs = nc.alloc_sbuf_tensor("bbcs", [128, 1], F32).ap()
    nbbcs = nc.alloc_sbuf_tensor("nbbcs", [16, 1], F32).ap()
    bdtpc = nc.alloc_sbuf_tensor("bdtpc", [128, 8], F32).ap()
    nbdtpc = nc.alloc_sbuf_tensor("nbdtpc", [128, 8], F32).ap()
    cents = nc.alloc_sbuf_tensor("cents", [128, 64], F32).ap()
    lngs = nc.alloc_sbuf_tensor("lngs", [8, 1024], F32).ap()
    lnbs = nc.alloc_sbuf_tensor("lnbs", [8, 1024], F32).ap()
    b1c = nc.alloc_sbuf_tensor("b1c", [128, 256], F32).ap()
    b2c = nc.alloc_sbuf_tensor("b2c", [128, 64], F32).ap()
    ones8 = nc.alloc_sbuf_tensor("ones8", [128, 8], F32).ap()
    iotaEs = nc.alloc_sbuf_tensor("iotaEs", [8, 1], F32).ap()
    toks = nc.alloc_sbuf_tensor("toks", [128, 16], I32).ap()
    slot32 = nc.alloc_sbuf_tensor("slot32", [128, 16], I32).ap()
    slotf = nc.alloc_sbuf_tensor("slotf", [128, 16], F32).ap()
    alphaff = nc.alloc_sbuf_tensor("alphaff", [128, 16], F32).ap()
    rowmaxs = nc.alloc_sbuf_tensor("rowmaxs", [128, 16], F32).ap()
    smallA = nc.alloc_sbuf_tensor("smallA", [128, 96], F32).ap()
    zeros4k = nc.alloc_sbuf_tensor("zeros4k", [128, 1024], F32).ap()
    gapb = nc.alloc_sbuf_tensor("gapb", [128, 256], F32).ap()
    epst = nc.alloc_sbuf_tensor("epst", [128, 1], F32).ap()

    PS = nc.alloc_psum_tensor("PS", [128, 4096], F32).ap()

    def bk(i):
        return PS[:, i * 512:(i + 1) * 512]

    def A4(i):
        return BIGA[:, i * 4096:(i + 1) * 4096]

    def B4(i):
        return BIGB[:, i * 4096:(i + 1) * 4096]

    def C4(i):
        return BIGC[:, i * 4096:(i + 1) * 4096]

    def vgap(cx):
        cx.op("vector", lambda v: v.tensor_copy(out=gapb[:], in_=zeros4k[:, 0:256]), inc=0)

    sem_names = ["bar", "dS", "dA", "dG", "vP", "sP", "tP", "gP"]
    sems = {s: nc.alloc_semaphore(name=s) for s in sem_names}
    cx = Ctx(nc, sems)

    def ph0(cx):
        cx.dma("scalar", ident[:], c_ident[:])
        cx.dma("scalar", ones8[:], c_ones[:])
        cx.dma("scalar", iotaEs[:], c_iotaE[:])
        cx.dma("scalar", zeros4k[:], c_zrow[0:1, 0:1024].partition_broadcast(128))
        cx.dma("scalar", ckt[:], ckl[:])
        cx.dma("scalar", alogs[:].rearrange("p (c n) -> p c n", c=8),
               alog.rearrange("(c p) n -> p c n", p=128))
        cx.dma("scalar", wcats[:], wcatl[:])
        cx.dma("scalar", wdtps[:], wdtp[:, :])
        cx.dma("scalar", bbcs[:], bbcl[:])
        cx.dma("scalar", bdtpc[:], bdtpl[:])
        cx.dma("scalar", dcols[:], dcol_in[:])
        cx.dma("scalar", cents[:], centl[:])
        cx.dma("scalar", lngs[:], lng[:, :])
        cx.dma("scalar", lnbs[:], lnb[:, :])
        cx.dma("scalar", b1c[:], b1l[:])
        cx.dma("scalar", b2c[:], b2l[:])
        cx.dma("scalar", toks[:], tokl[:])
        cx.dma("scalar", epst[:], c_eps[:])
        dA_consts = cx.cnt["dA"]

        cx.wait("gpsimd", "dA", dA_consts)
        for i in range((NSLOT + 128) // 128):
            cx.dma("gpsimd", slotx[i * 128:(i + 1) * 128, :], zeros4k[:, 0:1024])
            cx.dma("gpsimd", sloty[i * 128:(i + 1) * 128, :], zeros4k[:, 0:1024])

        cx.wait("vector", "dA", dA_consts)
        cx.op("vector", lambda v: v.tensor_scalar_mul(out=nbbcs[:], in0=bbcs[0:16, :], scalar1=-1.0))
        cx.op("vector", lambda v: v.tensor_scalar_mul(out=nbdtpc[:], in0=bdtpc[:], scalar1=-1.0))
        cx.wait("scalar", "dA", dA_consts)
        cx.op("scalar", lambda s: s.add(out=dp1[:], in_=dcols[:], add=1.0))
        cx.op("scalar", lambda s: s.activation(out=aexp[:], in_=alogs[:], func=AF.Exp))
        v0 = cx.cnt["vP"]
        s0 = cx.cnt["sP"]

        rowbuf = [A4(0), A4(1), A4(2), A4(3), B4(0), B4(1), C4(0), C4(1)]
        evac_eng = []
        evac_val = []
        tp_vals = []
        for t in range(32):
            tb = D2[:, 0:1024] if t % 2 == 0 else D3[:, 0:1024]
            if t >= 2:
                cx.wait("sync", "tP", tp_vals[t - 2])
            dv = cx.dma("sync", tb, x_in[t * 128:(t + 1) * 128, :])
            for c in range(NCH):
                idx = t * 8 + c
                bnk = idx % 4
                eng = "vector" if idx % 2 == 0 else "scalar"
                cx.wait("tensor", "dS", dv)
                cx.wait("tensor", "dA", dA_consts)
                if idx >= 4:
                    pe = evac_eng[idx - 4]
                    cx.wait("tensor", "vP" if pe == "vector" else "sP", evac_val[idx - 4])
                tv = cx.op("tensor", lambda te, tb=tb, c=c, bnk=bnk: te.transpose(
                    bk(bnk)[:, 0:128], tb[:, c * 128:(c + 1) * 128], ident[:]))
                cx.wait(eng, "tP", tv)
                if eng == "vector":
                    ev = cx.op("vector", lambda v, c=c, t=t, bnk=bnk: v.tensor_copy(
                        out=rowbuf[c][:, t * 128:(t + 1) * 128], in_=bk(bnk)[:, 0:128]))
                else:
                    ev = cx.op("scalar", lambda s, c=c, t=t, bnk=bnk: s.copy(
                        out=rowbuf[c][:, t * 128:(t + 1) * 128], in_=bk(bnk)[:, 0:128]))
                evac_eng.append(eng)
                evac_val.append(ev)
            tp_vals.append(cx.cnt["tP"])
        cx.wait("sync", "vP")
        cx.wait("sync", "sP")
        for c in range(NCH):
            cx.dma("sync", x_fm[c * 128:(c + 1) * 128, :], rowbuf[c])

    def ph1(cx):
        uA, uB = A4(2), A4(3)
        conv_done = []
        for c in range(NCH):
            xb = A4(0) if c % 2 == 0 else A4(1)
            if c >= 2:
                cx.wait("sync", "vP", conv_done[c - 2])
            dv = cx.dma("sync", xb, x_fm[c * 128:(c + 1) * 128, :])
            k0, k1, k2 = ckt[:, c:c + 1], ckt[:, 8 + c:9 + c], ckt[:, 16 + c:17 + c]
            cx.wait("vector", "dS", dv)
            if c >= 1:
                cx.wait("vector", "tP")
            cx.op("vector", lambda v, xb=xb, k0=k0: v.tensor_single_scalar(
                out=uA[:, 1:4096], in_=xb[:, 0:4095], scalar=k0, op=OP.mult))
            cx.op("vector", lambda v, xb=xb, k1=k1: v.scalar_tensor_tensor(
                out=uB[:, 1:4096], in0=xb[:, 1:4096], scalar=k1, in1=uA[:, 1:4096],
                op0=OP.mult, op1=OP.add))
            cx.op("vector", lambda v, xb=xb, k1=k1: v.tensor_single_scalar(
                out=uB[:, 0:1], in_=xb[:, 0:1], scalar=k1, op=OP.mult))
            cx.op("vector", lambda v, xb=xb, k2=k2: v.scalar_tensor_tensor(
                out=uA[:, 0:1], in0=xb[:, 1:2], scalar=k2, in1=uB[:, 0:1],
                op0=OP.mult, op1=OP.add))
            cx.op("vector", lambda v, xb=xb, k2=k2: v.scalar_tensor_tensor(
                out=uA[:, 1:4095], in0=xb[:, 2:4096], scalar=k2, in1=uB[:, 1:4095],
                op0=OP.mult, op1=OP.add))
            vv = cx.op("vector", lambda v: v.tensor_copy(out=uA[:, 4095:4096],
                                                         in_=uB[:, 4095:4096]))
            conv_done.append(vv)
            cx.wait("scalar", "vP", vv)
            sv = cx.op("scalar", lambda s: s.activation(out=uB[:], in_=uA[:], func=AF.Silu))
            if "u_dbg" in _DEV_TAPS:
                cx.wait("sync", "sP", sv)
                du = cx.dma("sync", u_dbg[c * 128:(c + 1) * 128, :], uB[:])
                cx.wait("scalar", "dS", du)
            cx.wait("tensor", "sP", sv)
            for lt in range(NLT):
                cx.op("tensor", lambda te, c=c, lt=lt: te.matmul(
                    PS[0:128, lt * 512:(lt + 1) * 512], wcats[:, c * 128:(c + 1) * 128],
                    uB[:, lt * 512:(lt + 1) * 512],
                    start=(c == 0), stop=(c == NCH - 1)), inc=1 if lt == NLT - 1 else 0)

    def ph2(cx):
        for lt in range(NLT):
            cx.op("scalar", lambda s, lt=lt: s.activation(
                out=D2[0:16, lt * 512:(lt + 1) * 512], in_=PS[0:16, lt * 512:(lt + 1) * 512],
                func=AF.Identity, scale=-1.0, bias=nbbcs[0:16, 0:1]))
            cx.op("scalar", lambda s, lt=lt: s.activation(
                out=D2[32:48, lt * 512:(lt + 1) * 512], in_=PS[32:48, lt * 512:(lt + 1) * 512],
                func=AF.Identity, bias=bbcs[32:48, 0:1]))
            cx.op("scalar", lambda s, lt=lt: s.activation(
                out=D1[0:64, lt * 512:(lt + 1) * 512], in_=PS[64:128, lt * 512:(lt + 1) * 512],
                func=AF.Identity, bias=bbcs[64:128, 0:1]))
        sv0 = cx.cnt["sP"]
        cx.wait("sync", "sP", sv0)
        cx.dma("sync", negBC_d[0:16, :], D2[0:16, :])
        cx.dma("sync", negBC_d[16:32, :], D2[32:48, :])
        sig_val = []
        for c in range(NCH):
            sgb = B4(0) if c % 2 == 0 else B4(1)
            lns = B4(1) if c % 2 == 0 else B4(0)
            if c >= 1:
                cx.wait("scalar", "dS")
            for lt in range(NLT):
                g = c * NLT + lt
                bnk = g % 4
                cx.wait("tensor", "sP", sv0 if g < 4 else sig_val[g - 4])
                tv = cx.op("tensor", lambda te, c=c, lt=lt, bnk=bnk: te.matmul(
                    bk(bnk)[0:128, :], wdtps[:, c * 128:(c + 1) * 128],
                    D1[0:64, lt * 512:(lt + 1) * 512], start=True, stop=True))
                cx.wait("scalar", "tP", tv)
                sg = cx.op("scalar", lambda s, c=c, lt=lt, bnk=bnk, sgb=sgb: s.activation(
                    out=sgb[:, lt * 512:(lt + 1) * 512], in_=bk(bnk),
                    func=AF.Sigmoid, scale=-1.0, bias=nbdtpc[:, c:c + 1]))
                sig_val.append(sg)
            cx.wait("vector", "sP")
            cv = cx.op("vector", lambda v, sgb=sgb: v.tensor_scalar_max(
                out=sgb[:], in0=sgb[:], scalar1=1e-38))
            cx.wait("scalar", "vP", cv)
            s2 = cx.op("scalar", lambda s, sgb=sgb, lns=lns: s.activation(
                out=lns[:], in_=sgb[:], func=AF.Ln))
            cx.wait("sync", "sP", s2)
            cx.dma("sync", lnsig_fm[c * 128:(c + 1) * 128, :], lns)

    def ph3(cx):
        lnsb, xlnsb = A4(0), A4(1)
        alb = [A4(2), A4(3)]
        brep = [B4(0), B4(1)]
        crep = [C4(0), C4(1)]
        bb = [D1, D2]
        scan_vals = []
        ym_vals = []
        br_dma = []
        cr_dma = []
        for c in range(NCH):
            cx.wait("sync", "sP")
            dln = cx.dma("sync", lnsb, lnsig_fm[c * 128:(c + 1) * 128, :])
            cx.wait("sync", "vP")
            dx0 = cx.dma("sync", bb[0][:], x_fm[c * 128:(c + 1) * 128, :])
            cx.wait("vector", "dS", max(dln, dx0))
            cx.op("vector", lambda v: v.tensor_mul(out=xlnsb[:], in0=bb[0][:], in1=lnsb[:]))
            for n in range(N):
                it = c * N + n
                p = it % 2
                if it >= 2:
                    cx.wait("sync", "vP", scan_vals[it - 2])
                dbr = cx.dma("sync", brep[p], negBC_d[n:n + 1, :].partition_broadcast(128))
                br_dma.append(dbr)
                if it >= 2:
                    cx.wait("scalar", "vP", ym_vals[it - 2])
                dcr = cx.dma("scalar", crep[p], negBC_d[N + n:N + n + 1, :].partition_broadcast(128))
                cr_dma.append(dcr)
                cx.wait("scalar", "dS", dln)
                if len(scan_vals) >= 2:
                    cx.wait("scalar", "vP", scan_vals[-2])
                sv = cx.op("scalar", lambda s, c=c, n=n, p=p: s.activation(
                    out=alb[p][:], in_=lnsb[:], func=AF.Exp,
                    scale=aexp[:, c * 16 + n:c * 16 + n + 1]))
                cx.wait("vector", "dS", dbr)
                if it >= 2:
                    cx.wait("vector", "tP", cx.cnt["tP"] - 1)
                cx.op("vector", lambda v, p=p: v.tensor_mul(out=bb[p][:], in0=brep[p][:],
                                                            in1=xlnsb[:]))
                cx.wait("vector", "sP", sv)
                scv = cx.op("vector", lambda v, p=p: v.tensor_tensor_scan(
                    out=D3[:], data0=alb[p][:], data1=bb[p][:], initial=0.0,
                    op0=OP.mult, op1=OP.add))
                scan_vals.append(scv)
                cx.wait("vector", "dA", dcr)
                ymv = cx.op("vector", lambda v, p=p: v.tensor_mul(out=bb[p][:], in0=D3[:],
                                                                  in1=crep[p][:]))
                ym_vals.append(ymv)
                cx.wait("tensor", "vP", ymv)
                for lt in range(NLT):
                    cx.op("tensor", lambda te, lt=lt, n=n, p=p: te.matmul(
                        bk(lt), ident[:], bb[p][:, lt * 512:(lt + 1) * 512],
                        start=(n == 0), stop=(n == N - 1)),
                        inc=1 if lt == NLT - 1 else 0)
            cx.wait("sync", "tP")
            dxr = cx.dma("sync", bb[0][:], x_fm[c * 128:(c + 1) * 128, :])
            cx.wait("vector", "dS", dxr)
            cx.wait("vector", "tP")
            rv = cx.op("vector", lambda v, c=c: v.scalar_tensor_tensor(
                out=xlnsb[:], in0=bb[0][:], scalar=dp1[:, c:c + 1], in1=PS[:, :],
                op0=OP.mult, op1=OP.add))
            cx.wait("sync", "vP", rv)
            cx.dma("sync", resid_fm[c * 128:(c + 1) * 128, :], xlnsb[:])
            cx.wait("tensor", "vP", rv)

    def ph4(cx):
        done = []
        for c in range(NCH):
            rb = A4(0) if c % 2 == 0 else A4(1)
            sq = A4(2) if c % 2 == 0 else A4(3)
            if c >= 2:
                cx.wait("sync", "tP", done[c - 2])
            dv = cx.dma("sync", rb, resid_fm[c * 128:(c + 1) * 128, :])
            cx.wait("scalar", "dS", dv)
            sv = cx.op("scalar", lambda s, rb=rb, sq=sq: s.activation(
                out=sq[:], in_=rb[:], func=AF.Square))
            cx.wait("tensor", "sP", sv)
            for lt in range(NLT):
                cx.op("tensor", lambda te, rb=rb, lt=lt, c=c: te.matmul(
                    PS[0:1, lt * 512:(lt + 1) * 512], ones8[:, 0:1],
                    rb[:, lt * 512:(lt + 1) * 512], start=(c == 0), stop=False), inc=0)
                cx.op("tensor", lambda te, sq=sq, lt=lt, c=c: te.matmul(
                    PS[32:33, lt * 512:(lt + 1) * 512], ones8[:, 0:1],
                    sq[:, lt * 512:(lt + 1) * 512],
                    start=(c == 0), stop=(c == NCH - 1)), inc=1 if lt == NLT - 1 else 0)
            done.append(cx.cnt["tP"])
        cx.wait("vector", "tP")
        cx.op("vector", lambda v: v.tensor_copy(out=B4(0)[0:1, :], in_=PS[0:1, :]))
        vq = cx.op("vector", lambda v: v.tensor_copy(out=D2[32:33, :], in_=PS[32:33, :]))
        cx.wait("sync", "vP", vq)
        dq = cx.dma("sync", B4(1)[0:1, :], D2[32:33, :])
        cx.wait("scalar", "vP", vq)
        cx.wait("scalar", "dS", dq)
        cx.op("scalar", lambda s: s.activation(out=C4(0)[0:1, :], in_=B4(0)[0:1, :],
                                               func=AF.Identity, scale=1.0 / D))
        cx.op("scalar", lambda s: s.activation(out=C4(1)[0:1, :], in_=B4(1)[0:1, :],
                                               func=AF.Identity, scale=1.0 / D))
        cx.op("scalar", lambda s: s.activation(out=B4(0)[0:1, :], in_=C4(0)[0:1, :],
                                               func=AF.Square))
        cx.wait("vector", "sP")
        cx.op("vector", lambda v: v.tensor_sub(out=B4(1)[0:1, :], in0=C4(1)[0:1, :],
                                               in1=B4(0)[0:1, :]))
        cx.wait("scalar", "vP")
        cx.op("scalar", lambda s: s.activation(out=C4(1)[0:1, :], in_=B4(1)[0:1, :],
                                               func=AF.Sqrt, bias=epst[0:1, 0:1]))
        cx.wait("vector", "sP")
        cx.op("vector", lambda v: v.reciprocal(out=D2[0:1, :], in_=C4(1)[0:1, :]))
        vgap(cx)
        cx.op("vector", lambda v: v.tensor_mul(out=B4(0)[0:1, :], in0=C4(0)[0:1, :],
                                               in1=D2[0:1, :]))
        cx.wait("sync", "vP")
        cx.dma("sync", rows_d[0:1, :], B4(0)[0:1, :])
        cx.dma("sync", rows_d[1:2, :], D2[0:1, :])

    def ph5(cx):
        cx.dma("sync", B4(0), rows_d[1:2, :].partition_broadcast(128))
        d1 = cx.dma("sync", B4(1), rows_d[0:1, :].partition_broadcast(128))
        mul_vals = []
        for c in range(NCH):
            rb = A4(0) if c % 2 == 0 else A4(1)
            xo = A4(2) if c % 2 == 0 else A4(3)
            if c >= 2:
                cx.wait("sync", "vP", mul_vals[c - 2] + 1)
            dv = cx.dma("sync", rb, resid_fm[c * 128:(c + 1) * 128, :])
            cx.wait("vector", "dS", dv)
            mv = cx.op("vector", lambda v, rb=rb: v.tensor_mul(out=D1[:], in0=rb[:], in1=B4(0)))
            mul_vals.append(mv)
            vv = cx.op("vector", lambda v, xo=xo: v.tensor_sub(out=xo[:], in0=D1[:], in1=B4(1)))
            cx.wait("sync", "vP", vv)
            cx.dma("sync", xln1_fm[c * 128:(c + 1) * 128, :], xo)

    def ph6(cx):
        ev_hist = []
        tp_hist = []
        wr_hist = []
        for t in range(32):
            tin = D2 if t % 2 == 0 else D3
            tout = BIGB[:, 0:1024] if t % 2 == 0 else BIGB[:, 1024:2048]
            if t >= 2:
                cx.wait("sync", "tP", tp_hist[t - 2])
            src = xln1_fm.rearrange("(c p) l -> p c l", p=128)[:, :, t * 128:(t + 1) * 128]
            dv = cx.dma("sync", tin[:, 0:1024].rearrange("p (c l) -> p c l", c=8), src)
            for c in range(NCH):
                idx = t * 8 + c
                bnk = idx % 2
                cx.wait("tensor", "dS", dv)
                if idx >= 2:
                    cx.wait("tensor", "vP", ev_hist[idx - 2])
                tv = cx.op("tensor", lambda te, tin=tin, c=c, bnk=bnk: te.transpose(
                    bk(bnk)[:, 0:128], tin[:, c * 128:(c + 1) * 128], ident[:]))
                cx.wait("vector", "tP", tv)
                if t >= 2 and c == 0:
                    cx.wait("vector", "dS", wr_hist[t - 2])
                ev = cx.op("vector", lambda v, tout=tout, c=c, bnk=bnk: v.tensor_copy(
                    out=tout[:, c * 128:(c + 1) * 128], in_=bk(bnk)[:, 0:128]))
                ev_hist.append(ev)
            tp_hist.append(cx.cnt["tP"])
            cx.wait("sync", "vP", ev_hist[-1])
            wr = cx.dma("sync", xln1_tm[t * 128:(t + 1) * 128, :], tout)
            wr_hist.append(wr)

    def ph7(cx):
        don = cx.dma("sync", B4(0)[0:8, 0:2048], c_onerow[0:1, 0:2048].partition_broadcast(8))
        for tt in range(NTT):
            g = BIGA[:, tt * 1024:(tt + 1) * 1024]
            dg = cx.idma(g, None, xln1_tm[:, :], IOffs(toks[:, tt:tt + 1], 0))
            for c in range(NCH):
                idx = tt * 8 + c
                cx.wait("tensor", "dG", dg)
                if idx >= 2:
                    cx.wait("tensor", "vP")
                tv = cx.op("tensor", lambda te, g=g, c=c, idx=idx: te.transpose(
                    bk(idx % 2)[:, 0:128], g[:, c * 128:(c + 1) * 128], ident[:]))
                cx.wait("vector", "tP", tv)
                cx.op("vector", lambda v, c=c, idx=idx: v.tensor_copy(
                    out=D1[:, 2048 + c * 128:2048 + (c + 1) * 128], in_=bk(idx % 2)[:, 0:128]))
            cx.wait("tensor", "vP")
            cx.wait("tensor", "sP")
            for c in range(NCH):
                cx.op("tensor", lambda te, c=c: te.matmul(
                    PS[0:8, 1024:1152], cents[:, c * 8:(c + 1) * 8],
                    D1[:, 2048 + c * 128:2048 + (c + 1) * 128],
                    start=(c == 0), stop=(c == NCH - 1)), inc=1 if c == NCH - 1 else 0)
            cx.wait("scalar", "tP")
            se = cx.op("scalar", lambda s: s.copy(out=D2[0:8, 1024:1152], in_=PS[0:8, 1024:1152]))
            cx.wait("tensor", "sP", se)
            cx.wait("tensor", "vP")
            tv2 = cx.op("tensor", lambda te: te.transpose(
                PS[:, 1536:1544], D2[0:8, 1024:1152], ident[0:8, 0:8]))
            cx.wait("vector", "tP", tv2)
            cx.op("vector", lambda v, tt=tt: v.tensor_copy(
                out=D2[:, tt * 8:(tt + 1) * 8], in_=PS[:, 1536:1544]))
            vv = cx.op("vector", lambda v, tt=tt: v.reduce_max(
                out=rowmaxs[:, tt:tt + 1], in_=PS[:, 1536:1544], axis=AX.X))
            cx.wait("scalar", "vP", vv)
            sb = cx.op("scalar", lambda s, tt=tt: s.copy(
                out=smallA[:, 32 + tt % 2:33 + tt % 2], in_=rowmaxs[:, tt:tt + 1]))
            cx.op("scalar", lambda s, tt=tt: s.activation(
                out=alphaff[:, tt:tt + 1], in_=rowmaxs[:, tt:tt + 1], func=AF.Sigmoid), inc=0)
            cx.wait("vector", "sP", sb)
            cx.op("vector", lambda v, tt=tt: v.tensor_single_scalar(
                out=smallA[:, 40:48], in_=D2[:, tt * 8:(tt + 1) * 8],
                scalar=smallA[:, 32 + tt % 2:33 + tt % 2], op=OP.is_equal))
            vgap(cx)
            cx.op("vector", lambda v: v.tensor_tensor_scan(
                out=smallA[:, 48:56], data0=ones8[:], data1=smallA[:, 40:48],
                initial=0.0, op0=OP.mult, op1=OP.add))
            vgap(cx)
            cx.op("vector", lambda v: v.tensor_single_scalar(
                out=smallA[:, 56:64], in_=smallA[:, 48:56], scalar=1.0, op=OP.is_equal))
            vgap(cx)
            vv2 = cx.op("vector", lambda v: v.tensor_mul(
                out=smallA[:, 64:72], in0=smallA[:, 40:48], in1=smallA[:, 56:64]))
            if "oh_dbg" in _DEV_TAPS and tt == 0:
                cx.wait("sync", "vP", vv2)
                cx.dma("sync", oh_dbg[:, :], smallA[:, 0:96])
                cx.dma("sync", aff_dbg[:, 0:8], D2[:, 0:8])
                cx.dma("sync", aff_dbg[:, 8:10], rowmaxs[:, 0:2])
                cx.wait("vector", "dS")
            cx.wait("tensor", "vP", vv2)
            tv3 = cx.op("tensor", lambda te: te.transpose(
                PS[0:8, 2048:2176], smallA[:, 64:72], ident[:]))
            cx.wait("vector", "tP", tv3)
            cx.op("vector", lambda v, tt=tt: v.tensor_copy(
                out=D3[0:8, tt * 128:(tt + 1) * 128], in_=PS[0:8, 2048:2176]))
        cx.wait("vector", "dS", don)
        cx.op("vector", lambda v: v.tensor_tensor_scan(
            out=B4(0)[0:8, 2048:4096], data0=B4(0)[0:8, 0:2048], data1=D3[0:8, 0:2048],
            initial=0.0, op0=OP.mult, op1=OP.add))
        vgap(cx)
        vv3 = cx.op("vector", lambda v: v.tensor_mul(
            out=B4(1)[0:8, 0:2048], in0=D3[0:8, 0:2048], in1=B4(0)[0:8, 2048:4096]))
        cx.wait("tensor", "vP", vv3)
        for q in range(4):
            cx.op("tensor", lambda te, q=q: te.matmul(
                PS[0:1, q * 512:(q + 1) * 512], ones8[0:8, 0:1],
                B4(1)[0:8, q * 512:(q + 1) * 512], start=True, stop=True), inc=0)
            cx.op("tensor", lambda te, q=q: te.matmul(
                PS[0:1, 2048 + q * 512:2048 + (q + 1) * 512], iotaEs[:],
                D3[0:8, q * 512:(q + 1) * 512], start=True, stop=True),
                inc=1 if q == 3 else 0)
        cx.wait("vector", "tP")
        cx.op("vector", lambda v: v.tensor_copy(out=C4(0)[0:1, 0:2048], in_=PS[0:1, 0:2048]))
        cx.op("vector", lambda v: v.tensor_copy(out=C4(0)[0:1, 2048:4096], in_=PS[0:1, 2048:4096]))
        rnk = C4(0)[0:1, 0:2048]
        eidr = C4(0)[0:1, 2048:4096]
        vgap(cx)
        cx.op("vector", lambda v: v.tensor_single_scalar(
            out=C4(1)[0:1, 0:2048], in_=eidr, scalar=float(CAP), op=OP.mult))
        vgap(cx)
        cx.op("vector", lambda v: v.tensor_add(
            out=C4(1)[0:1, 2048:4096], in0=C4(1)[0:1, 0:2048], in1=rnk))
        vgap(cx)
        cx.op("vector", lambda v: v.tensor_single_scalar(
            out=B4(1)[0:1, 2048:4096], in_=C4(1)[0:1, 2048:4096], scalar=1.0, op=OP.subtract))
        cx.op("vector", lambda v: v.tensor_single_scalar(
            out=C4(1)[0:1, 0:2048], in_=rnk, scalar=float(CAP) + 0.5, op=OP.is_ge))
        vgap(cx)
        cx.op("vector", lambda v: v.tensor_single_scalar(
            out=C4(1)[0:1, 2048:4096], in_=B4(1)[0:1, 2048:4096], scalar=float(NSLOT),
            op=OP.subtract))
        vgap(cx)
        cx.op("vector", lambda v: v.tensor_mul(
            out=B4(0)[0:1, 0:2048], in0=C4(1)[0:1, 0:2048], in1=C4(1)[0:1, 2048:4096]))
        vgap(cx)
        vslot = cx.op("vector", lambda v: v.tensor_sub(
            out=C4(0)[0:1, 2048:4096], in0=B4(1)[0:1, 2048:4096], in1=B4(0)[0:1, 0:2048]))
        slotF = C4(0)[0:1, 2048:4096]
        cx.wait("tensor", "vP", vslot)
        for tt in range(NTT):
            cx.wait("tensor", "vP")
            tv4 = cx.op("tensor", lambda te, tt=tt, slotF=slotF: te.transpose(
                PS[:, 2560:2561], slotF[0:1, tt * 128:(tt + 1) * 128], ident[0:1, 0:1]))
            cx.wait("vector", "tP", tv4)
            cx.op("vector", lambda v, tt=tt: v.tensor_copy(
                out=slotf[:, tt:tt + 1], in_=PS[:, 2560:2561]))
        vgap(cx)
        cx.op("vector", lambda v: v.tensor_copy(out=slot32[:], in_=slotf[:]))
        if "aff_dbg" in _DEV_TAPS:
            cx.wait("sync", "vP")
            cx.dma("sync", aff_dbg[:, :], D2[:, 0:128])
        if "route_dbg" in _DEV_TAPS:
            cx.wait("sync", "vP")
            cx.dma("sync", route_dbg[0:1, :], rnk)
            cx.dma("sync", route_dbg[1:2, :], eidr)
            cx.dma("sync", route_dbg[2:3, :], slotF)

    def ph8(cx):
        for tt in range(NTT):
            g = BIGA[:, tt * 1024:(tt + 1) * 1024]
            cx.wait("vector", "dG")
            cx.op("vector", lambda v, g=g: v.reduce_sum(
                out=smallA[:, 0:1], in_=g[:], axis=AX.X))
            cx.wait("scalar", "vP")
            cx.op("scalar", lambda s, g=g: s.activation(
                out=D1[:, 0:1024], in_=g[:], func=AF.Square))
            cx.wait("vector", "sP")
            cx.op("vector", lambda v: v.reduce_sum(
                out=smallA[:, 1:2], in_=D1[:, 0:1024], axis=AX.X))
            cx.wait("scalar", "vP")
            cx.op("scalar", lambda s: s.activation(out=smallA[:, 2:3], in_=smallA[:, 0:1],
                                                   func=AF.Copy, scale=1.0 / D))
            cx.op("scalar", lambda s: s.activation(out=smallA[:, 3:4], in_=smallA[:, 1:2],
                                                   func=AF.Copy, scale=1.0 / D))
            cx.op("scalar", lambda s: s.activation(out=smallA[:, 4:5], in_=smallA[:, 2:3],
                                                   func=AF.Square))
            cx.wait("vector", "sP")
            cx.op("vector", lambda v: v.tensor_sub(out=smallA[:, 5:6], in0=smallA[:, 3:4],
                                                   in1=smallA[:, 4:5]))
            cx.wait("scalar", "vP")
            cx.op("scalar", lambda s: s.activation(out=smallA[:, 6:7], in_=smallA[:, 5:6],
                                                   func=AF.Sqrt, bias=epst[:, 0:1]))
            cx.wait("vector", "sP")
            vv = cx.op("vector", lambda v: v.reciprocal(out=smallA[:, 7:8], in_=smallA[:, 6:7]))
            cx.wait("scalar", "vP", vv)
            cx.op("scalar", lambda s: s.copy(out=smallA[:, 8:9], in_=smallA[:, 2:3]))
            sb = cx.op("scalar", lambda s: s.copy(out=smallA[:, 9:10], in_=smallA[:, 7:8]))
            cx.wait("vector", "sP", sb)
            vg = cx.op("vector", lambda v, g=g: v.tensor_scalar(
                out=D1[:, 1024:2048], in0=g[:], scalar1=smallA[:, 8:9],
                scalar2=smallA[:, 9:10], op0=OP.subtract, op1=OP.mult))
            cx.wait("tensor", "vP", vg)
            for hh in range(2):
                cx.op("tensor", lambda te, tt=tt, hh=hh: te.matmul(
                    bk(hh), D3[0:8, tt * 128:(tt + 1) * 128],
                    lngs[:, hh * 512:(hh + 1) * 512], start=True, stop=True), inc=0)
                cx.op("tensor", lambda te, tt=tt, hh=hh: te.matmul(
                    bk(2 + hh), D3[0:8, tt * 128:(tt + 1) * 128],
                    lnbs[:, hh * 512:(hh + 1) * 512], start=True, stop=True),
                    inc=1 if hh == 1 else 0)
            cx.wait("vector", "tP")
            for hh in range(2):
                cx.op("vector", lambda v, hh=hh: v.tensor_mul(
                    out=D1[:, 2048 + hh * 512:2048 + (hh + 1) * 512],
                    in0=D1[:, 1024 + hh * 512:1024 + (hh + 1) * 512], in1=bk(hh)))
            for hh in range(2):
                cx.op("vector", lambda v, hh=hh: v.tensor_add(
                    out=D1[:, 3072 + hh * 512:3072 + (hh + 1) * 512],
                    in0=D1[:, 2048 + hh * 512:2048 + (hh + 1) * 512], in1=bk(2 + hh)))
            vv2 = cx.cnt["vP"]
            cx.wait("gpsimd", "vP", vv2)
            cx.idma(slotx[:, :], IOffs(slot32[:, tt:tt + 1], 0), D1[:, 3072:4096], None)

    def ph9(cx):
        NST = CAP // 128
        BIGAb = BIGA.bitcast(BF16)
        BIGCb = BIGC.bitcast(BF16)
        D2b = D2.bitcast(BF16)

        def whi(p):
            return BIGCb[:, p * 8192:p * 8192 + 4096]

        def wlo(p):
            return BIGCb[:, p * 8192 + 4096:p * 8192 + 8192]

        h_hi = BIGAb[:, 0:16384]
        h_lo = BIGAb[:, 16384:32768]
        xs_hi = D2b[:, 0:4096]
        xs_lo = D2b[:, 4096:8192]
        wg = []
        for e in range(E):
            cx.wait("gpsimd", "tP")
            cx.wait("gpsimd", "vP")
            dgl = 0
            for st in range(NST):
                dgl = cx.dma("gpsimd", D3[:, st * 1024:(st + 1) * 1024],
                             slotx[e * CAP + st * 128:e * CAP + (st + 1) * 128, :])
            for st in range(NST):
                for c in range(NCH):
                    idx = st * 8 + c
                    cx.wait("tensor", "dG", dgl)
                    if idx >= 2:
                        cx.wait("tensor", "vP")
                    tv = cx.op("tensor", lambda te, st=st, c=c, idx=idx: te.transpose(
                        bk(idx % 2)[:, 0:128],
                        D3[:, st * 1024 + c * 128:st * 1024 + (c + 1) * 128], ident[:]))
                    cx.wait("vector", "tP", tv)
                    cx.op("vector", lambda v, st=st, c=c, idx=idx: v.tensor_copy(
                        out=D1[:, c * CAP + st * 128:c * CAP + (st + 1) * 128],
                        in_=bk(idx % 2)[:, 0:128]))
            cx.op("vector", lambda v: v.tensor_copy(out=xs_hi[:, :], in_=D1[:]))
            vxs = cx.op("vector", lambda v: v.tensor_tensor(
                out=xs_lo[:, :], in0=D1[:], in1=xs_hi[:, :], op=OP.subtract))
            for q in range(8):
                p = q % 2
                if len(wg) >= 2:
                    cx.wait("sync", "tP", wg[-2])
                dw = cx.dma("sync", B4(p).rearrange("p (c f) -> p c f", c=8),
                            w1[e, :, q * 512:(q + 1) * 512].rearrange("(c p) f -> p c f", p=128))
                cx.wait("vector", "dS", dw)
                cx.op("vector", lambda v, p=p: v.tensor_copy(out=whi(p)[:, :], in_=B4(p)))
                wsp = cx.op("vector", lambda v, p=p: v.tensor_tensor(
                    out=wlo(p)[:, :], in0=B4(p), in1=whi(p)[:, :], op=OP.subtract))
                cx.wait("tensor", "vP", max(vxs, wsp))
                for fc in range(4):
                    fi = q * 4 + fc
                    hb = 2 + fi % 2
                    if fi >= 2:
                        cx.wait("tensor", "vP")
                    for c in range(NCH):
                        for ph_, lhs, rhs in ((0, whi(p), xs_hi), (1, whi(p), xs_lo),
                                              (2, wlo(p), xs_hi)):
                            cx.op("tensor", lambda te, lhs=lhs, rhs=rhs, c=c, fc=fc, hb=hb,
                                  ph_=ph_: te.matmul(
                                      bk(hb), lhs[:, c * 512 + fc * 128:c * 512 + (fc + 1) * 128],
                                      rhs[:, c * 512:(c + 1) * 512],
                                      start=(c == 0 and ph_ == 0),
                                      stop=(c == NCH - 1 and ph_ == 2)),
                                  inc=1 if (c == NCH - 1 and ph_ == 2) else 0)
                    cx.wait("scalar", "tP")
                    htmp = D1[:, (fi % 2) * 512:(fi % 2) * 512 + 512]
                    if fi >= 2:
                        cx.wait("scalar", "vP")
                    cx.op("scalar", lambda s, e=e, fi=fi, hb=hb, htmp=htmp: s.activation(
                        out=htmp, in_=bk(hb),
                        func=AF.Relu, bias=b1c[:, e * 32 + fi:e * 32 + fi + 1]))
                    sh = cx.op("scalar", lambda s, fi=fi, htmp=htmp: s.copy(
                        out=h_hi[:, fi * 512:(fi + 1) * 512], in_=htmp))
                    cx.wait("vector", "sP", sh)
                    cx.op("vector", lambda v, fi=fi, htmp=htmp: v.tensor_tensor(
                        out=h_lo[:, fi * 512:(fi + 1) * 512], in0=htmp,
                        in1=h_hi[:, fi * 512:(fi + 1) * 512], op=OP.subtract))
                wg.append(cx.cnt["tP"])
            smm1 = cx.cnt["vP"]
            if "h_dbg" in _DEV_TAPS and e == 0:
                cx.wait("sync", "vP", smm1)
                cx.wait("sync", "sP")
                cx.dma("sync", h_dbg[:, 0:8192], BIGA[:, 0:8192])
                cx.dma("sync", h_dbg[:, 8192:16384], BIGA[:, 8192:16384])
                cx.wait("tensor", "dS")
            for fi in range(32):
                q, fc = fi // 4, fi % 4
                if fc == 0:
                    p = q % 2
                    if len(wg) >= 2:
                        cx.wait("sync", "tP", wg[-2])
                    dw2 = cx.dma("sync", B4(p).rearrange("p (fc d) -> p fc d", fc=4),
                                 w2[e, q * 512:(q + 1) * 512, :]
                                 .rearrange("(fc p) d -> p fc d", p=128))
                    cx.wait("vector", "dS", dw2)
                    cx.op("vector", lambda v, p=p: v.tensor_copy(out=whi(p)[:, :], in_=B4(p)))
                    wsp2 = cx.op("vector", lambda v, p=p: v.tensor_tensor(
                        out=wlo(p)[:, :], in0=B4(p), in1=whi(p)[:, :], op=OP.subtract))
                    cx.wait("tensor", "vP", max(smm1, wsp2))
                for do in range(8):
                    for ph_, lhs, rhsof in ((0, whi(p), 0), (1, whi(p), 16384),
                                            (2, wlo(p), 0)):
                        cx.op("tensor", lambda te, lhs=lhs, fc=fc, do=do, fi=fi, ph_=ph_,
                              rhsof=rhsof: te.matmul(
                                  bk(do), lhs[:, fc * 1024 + do * 128:fc * 1024 + (do + 1) * 128],
                                  BIGAb[:, rhsof + fi * 512:rhsof + (fi + 1) * 512],
                                  start=(fi == 0 and ph_ == 0),
                                  stop=(fi == 31 and ph_ == 2)),
                              inc=1 if (do == 7 and ph_ == 2) else 0)
                if fc == 3:
                    wg.append(cx.cnt["tP"])
            cx.wait("scalar", "tP")
            for do in range(8):
                cx.op("scalar", lambda s, e=e, do=do: s.activation(
                    out=D1[:, do * 512:(do + 1) * 512], in_=bk(do),
                    func=AF.Identity, bias=b2c[:, e * 8 + do:e * 8 + do + 1]))
            sy2 = cx.cnt["sP"]
            for st in range(NST):
                for do in range(8):
                    idx = st * 8 + do
                    cx.wait("tensor", "sP", sy2)
                    if idx >= 2:
                        cx.wait("tensor", "vP")
                    t2 = cx.op("tensor", lambda te, st=st, do=do, idx=idx: te.transpose(
                        bk(idx % 2)[:, 0:128],
                        D1[:, do * 512 + st * 128:do * 512 + (st + 1) * 128], ident[:]))
                    cx.wait("vector", "tP", t2)
                    if e >= 1 and idx == 0:
                        cx.wait("vector", "dG")
                    cx.op("vector", lambda v, st=st, do=do, idx=idx: v.tensor_copy(
                        out=D3[:, st * 1024 + do * 128:st * 1024 + (do + 1) * 128],
                        in_=bk(idx % 2)[:, 0:128]))
            vy2 = cx.cnt["vP"]
            cx.wait("gpsimd", "vP", vy2)
            for st in range(NST):
                cx.dma("gpsimd", sloty[e * CAP + st * 128:e * CAP + (st + 1) * 128, :],
                       D3[:, st * 1024:(st + 1) * 1024])

    def ph10(cx):
        stt_vals = []
        out_wr = []
        for tt in range(NTT):
            fb = D1 if tt % 2 == 0 else D2
            if tt >= 2:
                cx.wait("gpsimd", "vP", stt_vals[tt - 2])
            dg1 = cx.idma(fb[:, 0:1024], None, sloty[:, :], IOffs(slot32[:, tt:tt + 1], 0))
            dg2 = cx.idma(fb[:, 1024:2048], None, xln1_tm[:, :], IOffs(toks[:, tt:tt + 1], 0))
            cx.wait("vector", "dG", dg2)
            cx.op("vector", lambda v, fb=fb, tt=tt: v.tensor_single_scalar(
                out=D3[:, 0:1024], in_=fb[:, 0:1024], scalar=alphaff[:, tt:tt + 1],
                op=OP.mult))
            vgap(cx)
            sv0 = cx.op("vector", lambda v, fb=fb: v.scalar_tensor_tensor(
                out=D3[:, 1024:2048], in0=fb[:, 1024:2048], scalar=2.0, in1=D3[:, 0:1024],
                op0=OP.mult, op1=OP.add))
            stt_vals.append(sv0)
            vgap(cx)
            cx.op("vector", lambda v: v.reduce_sum(out=smallA[:, 16:17],
                                                   in_=D3[:, 1024:2048], axis=AX.X))
            cx.wait("scalar", "vP")
            cx.op("scalar", lambda s: s.activation(out=D3[:, 0:1024], in_=D3[:, 1024:2048],
                                                   func=AF.Square))
            cx.wait("vector", "sP")
            cx.op("vector", lambda v: v.reduce_sum(out=smallA[:, 17:18],
                                                   in_=D3[:, 0:1024], axis=AX.X))
            cx.wait("scalar", "vP")
            cx.op("scalar", lambda s: s.activation(out=smallA[:, 18:19], in_=smallA[:, 16:17],
                                                   func=AF.Copy, scale=1.0 / D))
            cx.op("scalar", lambda s: s.activation(out=smallA[:, 19:20], in_=smallA[:, 17:18],
                                                   func=AF.Copy, scale=1.0 / D))
            cx.op("scalar", lambda s: s.activation(out=smallA[:, 20:21], in_=smallA[:, 18:19],
                                                   func=AF.Square))
            cx.wait("vector", "sP")
            cx.op("vector", lambda v: v.tensor_sub(out=smallA[:, 21:22], in0=smallA[:, 19:20],
                                                   in1=smallA[:, 20:21]))
            cx.wait("scalar", "vP")
            cx.op("scalar", lambda s: s.activation(out=smallA[:, 22:23], in_=smallA[:, 21:22],
                                                   func=AF.Sqrt, bias=epst[:, 0:1]))
            cx.wait("vector", "sP")
            vr = cx.op("vector", lambda v: v.reciprocal(out=smallA[:, 23:24],
                                                        in_=smallA[:, 22:23]))
            cx.wait("scalar", "vP", vr)
            cx.op("scalar", lambda s: s.copy(out=smallA[:, 24:25], in_=smallA[:, 18:19]))
            sb = cx.op("scalar", lambda s: s.copy(out=smallA[:, 25:26], in_=smallA[:, 23:24]))
            cx.wait("vector", "sP", sb)
            ob = D3[:, 2048:3072] if tt % 2 == 0 else D3[:, 3072:4096]
            if tt >= 2:
                cx.wait("vector", "dS", out_wr[tt - 2])
            vv = cx.op("vector", lambda v, ob=ob: v.tensor_scalar(
                out=ob, in0=D3[:, 1024:2048], scalar1=smallA[:, 24:25],
                scalar2=smallA[:, 25:26], op0=OP.subtract, op1=OP.mult))
            cx.wait("sync", "vP", vv)
            wr = cx.dma("sync", out_rows[tt * 128:(tt + 1) * 128, :], ob)
            out_wr.append(wr)

    phases = [ph0, ph1, ph2, ph3, ph4, ph5, ph6, ph7, ph8, ph9, ph10]
    phases = phases[:_LAST_PHASE + 1]

    def engine_program(name):
        def run(eng):
            cx.begin(name, eng)
            for ph in phases:
                ph(cx)
                cx.barrier()
        return run

    with nc.Block() as block:
        block.sync(engine_program("sync"))
        block.scalar(engine_program("scalar"))
        block.vector(engine_program("vector"))
        block.tensor(engine_program("tensor"))
        block.gpsimd(engine_program("gpsimd"))

    return nc


_CACHE = {}


def _get_nc():
    key = (_LAST_PHASE, tuple(_DEV_TAPS))
    if key not in _CACHE:
        _CACHE[key] = _build()
    return _CACHE[key]


def _in_maps(inputs):
    xfull = np.ascontiguousarray(inputs["L0_attn_x"], dtype=np.float32)
    consts = {
        "c_ident": np.eye(128, dtype=np.float32),
        "c_ones": np.ones((128, 8), np.float32),
        "c_iotaE": np.arange(8, dtype=np.float32).reshape(8, 1),
        "c_onerow": np.ones((1, 4096), np.float32),
        "c_zrow": np.zeros((1, 4096), np.float32),
        "c_eps": np.full((128, 1), 1e-5, np.float32),
    }
    wmap = {
        "A_log": "L0_attn_A_log", "W_dtp": "L0_attn_W_dtp",
        "ln_g": "L0_ffn_ln_g", "ln_b": "L0_ffn_ln_b", "w1": "L0_ffn_w1",
        "w2": "L0_ffn_w2",
    }
    weights = {k: np.ascontiguousarray(inputs[v], dtype=np.float32) for k, v in wmap.items()}
    f32c = lambda a: np.ascontiguousarray(a, dtype=np.float32)
    weights["ck_l"] = f32c(np.asarray(inputs["L0_attn_conv_kernel"], np.float32)
                           .reshape(3, 8, 128).transpose(2, 0, 1).reshape(128, 24))
    _wc = np.zeros((1024, 128), np.float32)
    _wbc_np = np.asarray(inputs["L0_attn_W_bc"], np.float32)
    _wc[:, 0:16] = _wbc_np[:, 0:16]
    _wc[:, 32:48] = _wbc_np[:, 16:32]
    _wc[:, 64:128] = np.asarray(inputs["L0_attn_W_dt"], np.float32)
    weights["wcat_l"] = f32c(_wc.reshape(8, 128, 128).transpose(1, 0, 2).reshape(128, 1024))
    _bl = np.zeros((128, 1), np.float32)
    _bl[0:16, 0] = np.asarray(inputs["L0_attn_b_bc"], np.float32)[0:16]
    _bl[32:48, 0] = np.asarray(inputs["L0_attn_b_bc"], np.float32)[16:32]
    _bl[64:128, 0] = np.asarray(inputs["L0_attn_b_dt"], np.float32)
    weights["b_bc_l"] = _bl
    weights["b_dtp_l"] = f32c(np.asarray(inputs["L0_attn_b_dtp"], np.float32).reshape(8, 128).T)
    weights["Dco_l"] = f32c(np.asarray(inputs["L0_attn_Dcoeff"], np.float32).reshape(8, 128).T)
    weights["cent_l"] = f32c(np.asarray(inputs["L0_ffn_centroids"], np.float32)
                             .reshape(8, 8, 128).transpose(2, 1, 0).reshape(128, 64))
    weights["b1_l"] = f32c(np.asarray(inputs["L0_ffn_b1"], np.float32)
                           .reshape(8, 32, 128).transpose(2, 0, 1).reshape(128, 256))
    weights["b2_l"] = f32c(np.asarray(inputs["L0_ffn_b2"], np.float32)
                           .reshape(8, 8, 128).transpose(2, 0, 1).reshape(128, 64))
    maps = []
    for j in range(8):
        b, h = j // 2, j % 2
        m = dict(consts)
        m.update(weights)
        m["x"] = xfull[b]
        m["tok_l"] = np.ascontiguousarray(
            (h * TPC + np.arange(TPC)).astype(np.int32).reshape(16, 128).T)
        maps.append(m)
    return maps


def kernel(**inputs):
    nc = _get_nc()
    res = run_bass_kernel_spmd(nc, _in_maps(inputs), core_ids=list(range(8)))
    out = np.empty((B, L, D), np.float32)
    for j in range(8):
        b, h = j // 2, j % 2
        out[b, h * TPC:(h + 1) * TPC] = res.results[j]["out_rows"]
    return out


# version 04
# speedup vs baseline: 1.0257x; 1.0257x over previous
import numpy as np
import concourse.bass as bass
import concourse.mybir as mybir
from concourse.bass_utils import run_bass_kernel_spmd

F32 = mybir.dt.float32
BF16 = mybir.dt.bfloat16
I32 = mybir.dt.int32
AF = mybir.ActivationFunctionType
OP = mybir.AluOpType
AX = mybir.AxisListType
IOffs = bass.IndirectOffsetOnAxis

B, L, D, N = 4, 4096, 1024, 16
DT_RANK = 64
E, FF, CAP = 8, 4096, 512
NSLOT = E * CAP
TPC = 2048
NTT = TPC // 128
NCH = D // 128
NLT = L // 512

_DEV_TAPS = ()
_LAST_PHASE = 10

_ENGINES = ("sync", "scalar", "vector", "tensor", "gpsimd")


class Ctx:

    def __init__(self, nc, sems):
        self.nc = nc
        self.sems = sems
        self.active = None
        self.eng = None
        self.cnt = {k: 0 for k in sems}
        self.queue_sem = {"sync": "dS", "scalar": "dA", "gpsimd": "dG"}

    def begin(self, engine, eng_handle):
        self.active = engine
        self.eng = eng_handle
        self.cnt = {k: 0 for k in self.sems}

    def dma(self, queue, out, in_, **kw):
        s = self.queue_sem[queue]
        self.cnt[s] += 16
        if self.active == queue:
            self.eng.dma_start(out=out, in_=in_, **kw).then_inc(self.sems[s], 16)
        return self.cnt[s]

    def idma(self, out, out_off, in_, in_off):
        self.cnt["dG"] += 16
        if self.active == "gpsimd":
            self.eng.indirect_dma_start(out=out, out_offset=out_off, in_=in_,
                                        in_offset=in_off).then_inc(self.sems["dG"], 16)
        return self.cnt["dG"]

    def op(self, engine, fn, inc=1):
        sem = {"scalar": "sP", "vector": "vP", "tensor": "tP", "gpsimd": "gP"}[engine]
        self.cnt[sem] += inc
        if self.active == engine:
            instr = fn(self.eng)
            if inc:
                instr.then_inc(self.sems[sem], inc)
        return self.cnt[sem]

    def wait(self, engine, sem, val=None):
        v = self.cnt[sem] if val is None else val
        if self.active == engine and v > 0:
            self.eng.wait_ge(self.sems[sem], v)

    def barrier(self):
        for s in ("dS", "dA", "dG", "vP", "sP", "tP", "gP"):
            for e in _ENGINES:
                self.wait(e, s)
        self.cnt["bar"] += 5
        if self.active is not None:
            self.eng.sem_inc(self.sems["bar"], 1)
            self.eng.wait_ge(self.sems["bar"], self.cnt["bar"])


def _build():
    nc = bass.Bass("TRN2", target_bir_lowering=False)

    def din(name, shape, dt=F32):
        return nc.dram_tensor(name, shape, dt, kind="ExternalInput").ap()

    def dscratch(name, shape, dt=F32):
        kind = "ExternalOutput" if name in _DEV_TAPS else "Internal"
        return nc.dram_tensor(name, shape, dt, kind=kind).ap()

    x_in = din("x", [L, D])
    tokl = din("tok_l", [128, 16], I32)
    ckl = din("ck_l", [128, 24])
    alog = din("A_log", [D, N])
    wcatl = din("wcat_l", [128, 1024])
    bbcl = din("b_bc_l", [128, 1])
    dcol_in = din("Dco_l", [128, 8])


    wdtp = din("W_dtp", [DT_RANK, D])
    bdtpl = din("b_dtp_l", [128, 8])
    centl = din("cent_l", [128, 64])
    lng = din("ln_g", [E, D])
    lnb = din("ln_b", [E, D])
    w1 = din("w1", [E, D, FF])
    b1l = din("b1_l", [128, 256])
    w2 = din("w2", [E, FF, D])
    b2l = din("b2_l", [128, 64])
    c_ident = din("c_ident", [128, 128])
    c_ones = din("c_ones", [128, 8])
    c_iotaE = din("c_iotaE", [8, 1])
    c_onerow = din("c_onerow", [1, 4096])
    c_zrow = din("c_zrow", [1, 4096])
    c_eps = din("c_eps", [128, 1])

    out_rows = nc.dram_tensor("out_rows", [TPC, D], F32, kind="ExternalOutput").ap()

    x_fm = dscratch("x_fm", [D, L])
    lnsig_fm = dscratch("lnsig_fm", [D, L])
    resid_fm = dscratch("resid_fm", [D, L])
    xln1_fm = dscratch("xln1_fm", [D, L])
    xln1_tm = dscratch("xln1_tm", [L, D])
    negBC_d = dscratch("negBC_d", [2 * N, L])
    rows_d = dscratch("rows_d", [2, L])
    u_dbg = dscratch("u_dbg", [D, L])
    route_dbg = dscratch("route_dbg", [4, TPC])
    aff_dbg = dscratch("aff_dbg", [128, 128])
    oh_dbg = dscratch("oh_dbg", [128, 96])
    h_dbg = dscratch("h_dbg", [128, 16384])
    slotx = dscratch("slotx", [NSLOT + 128, D])
    sloty = dscratch("sloty", [NSLOT + 128, D])

    BIGA = nc.alloc_sbuf_tensor("BIGA", [128, 16384], F32).ap()
    BIGB = nc.alloc_sbuf_tensor("BIGB", [128, 8192], F32).ap()
    BIGC = nc.alloc_sbuf_tensor("BIGC", [128, 8192], F32).ap()
    D1 = nc.alloc_sbuf_tensor("D1", [128, 4096], F32).ap()
    D2 = nc.alloc_sbuf_tensor("D2", [128, 4096], F32).ap()
    D3 = nc.alloc_sbuf_tensor("D3", [128, 4096], F32).ap()
    ident = nc.alloc_sbuf_tensor("ident", [128, 128], F32).ap()
    aexp = nc.alloc_sbuf_tensor("aexp", [128, 128], F32).ap()
    ckt = nc.alloc_sbuf_tensor("ckt", [128, 24], F32).ap()
    dp1 = nc.alloc_sbuf_tensor("dp1", [128, 8], F32).ap()
    dcols = nc.alloc_sbuf_tensor("dcols", [128, 8], F32).ap()
    alogs = nc.alloc_sbuf_tensor("alogs", [128, 128], F32).ap()
    wcats = nc.alloc_sbuf_tensor("wcats", [128, 1024], F32).ap()
    wdtps = nc.alloc_sbuf_tensor("wdtps", [64, 1024], F32).ap()
    bbcs = nc.alloc_sbuf_tensor("bbcs", [128, 1], F32).ap()
    nbbcs = nc.alloc_sbuf_tensor("nbbcs", [16, 1], F32).ap()
    bdtpc = nc.alloc_sbuf_tensor("bdtpc", [128, 8], F32).ap()
    nbdtpc = nc.alloc_sbuf_tensor("nbdtpc", [128, 8], F32).ap()
    cents = nc.alloc_sbuf_tensor("cents", [128, 64], F32).ap()
    lngs = nc.alloc_sbuf_tensor("lngs", [8, 1024], F32).ap()
    lnbs = nc.alloc_sbuf_tensor("lnbs", [8, 1024], F32).ap()
    b1c = nc.alloc_sbuf_tensor("b1c", [128, 256], F32).ap()
    b2c = nc.alloc_sbuf_tensor("b2c", [128, 64], F32).ap()
    ones8 = nc.alloc_sbuf_tensor("ones8", [128, 8], F32).ap()
    iotaEs = nc.alloc_sbuf_tensor("iotaEs", [8, 1], F32).ap()
    toks = nc.alloc_sbuf_tensor("toks", [128, 16], I32).ap()
    slot32 = nc.alloc_sbuf_tensor("slot32", [128, 16], I32).ap()
    slotf = nc.alloc_sbuf_tensor("slotf", [128, 16], F32).ap()
    alphaff = nc.alloc_sbuf_tensor("alphaff", [128, 16], F32).ap()
    rowmaxs = nc.alloc_sbuf_tensor("rowmaxs", [128, 16], F32).ap()
    smallA = nc.alloc_sbuf_tensor("smallA", [128, 96], F32).ap()
    zeros4k = nc.alloc_sbuf_tensor("zeros4k", [128, 1024], F32).ap()
    gapb = nc.alloc_sbuf_tensor("gapb", [128, 256], F32).ap()
    epst = nc.alloc_sbuf_tensor("epst", [128, 1], F32).ap()

    PS = nc.alloc_psum_tensor("PS", [128, 4096], F32).ap()

    def bk(i):
        return PS[:, i * 512:(i + 1) * 512]

    def A4(i):
        return BIGA[:, i * 4096:(i + 1) * 4096]

    def B4(i):
        return BIGB[:, i * 4096:(i + 1) * 4096]

    def C4(i):
        return BIGC[:, i * 4096:(i + 1) * 4096]

    def vgap(cx):
        cx.op("vector", lambda v: v.tensor_copy(out=gapb[:], in_=zeros4k[:, 0:256]), inc=0)

    sem_names = ["bar", "dS", "dA", "dG", "vP", "sP", "tP", "gP"]
    sems = {s: nc.alloc_semaphore(name=s) for s in sem_names}
    cx = Ctx(nc, sems)

    def ph0(cx):
        cx.dma("scalar", ident[:], c_ident[:])
        cx.dma("scalar", ones8[:], c_ones[:])
        cx.dma("scalar", iotaEs[:], c_iotaE[:])
        cx.dma("scalar", zeros4k[:], c_zrow[0:1, 0:1024].partition_broadcast(128))
        cx.dma("scalar", ckt[:], ckl[:])
        cx.dma("scalar", alogs[:].rearrange("p (c n) -> p c n", c=8),
               alog.rearrange("(c p) n -> p c n", p=128))
        cx.dma("scalar", wcats[:], wcatl[:])
        cx.dma("scalar", wdtps[:], wdtp[:, :])
        cx.dma("scalar", bbcs[:], bbcl[:])
        cx.dma("scalar", bdtpc[:], bdtpl[:])
        cx.dma("scalar", dcols[:], dcol_in[:])
        cx.dma("scalar", cents[:], centl[:])
        cx.dma("scalar", lngs[:], lng[:, :])
        cx.dma("scalar", lnbs[:], lnb[:, :])
        cx.dma("scalar", b1c[:], b1l[:])
        cx.dma("scalar", b2c[:], b2l[:])
        cx.dma("scalar", toks[:], tokl[:])
        cx.dma("scalar", epst[:], c_eps[:])
        dA_consts = cx.cnt["dA"]

        cx.wait("gpsimd", "dA", dA_consts)
        for i in range((NSLOT + 128) // 128):
            cx.dma("gpsimd", slotx[i * 128:(i + 1) * 128, :], zeros4k[:, 0:1024])
            cx.dma("gpsimd", sloty[i * 128:(i + 1) * 128, :], zeros4k[:, 0:1024])

        cx.wait("vector", "dA", dA_consts)
        cx.op("vector", lambda v: v.tensor_scalar_mul(out=nbbcs[:], in0=bbcs[0:16, :], scalar1=-1.0))
        cx.op("vector", lambda v: v.tensor_scalar_mul(out=nbdtpc[:], in0=bdtpc[:], scalar1=-1.0))
        cx.wait("scalar", "dA", dA_consts)
        cx.op("scalar", lambda s: s.add(out=dp1[:], in_=dcols[:], add=1.0))
        cx.op("scalar", lambda s: s.activation(out=aexp[:], in_=alogs[:], func=AF.Exp))
        v0 = cx.cnt["vP"]
        s0 = cx.cnt["sP"]

        rowbuf = [A4(0), A4(1), A4(2), A4(3), B4(0), B4(1), C4(0), C4(1)]
        evac_eng = []
        evac_val = []
        tp_vals = []
        for t in range(32):
            tb = D2[:, 0:1024] if t % 2 == 0 else D3[:, 0:1024]
            if t >= 2:
                cx.wait("sync", "tP", tp_vals[t - 2])
            dv = cx.dma("sync", tb, x_in[t * 128:(t + 1) * 128, :])
            for c in range(NCH):
                idx = t * 8 + c
                bnk = idx % 4
                eng = "vector" if idx % 2 == 0 else "scalar"
                cx.wait("tensor", "dS", dv)
                cx.wait("tensor", "dA", dA_consts)
                if idx >= 4:
                    pe = evac_eng[idx - 4]
                    cx.wait("tensor", "vP" if pe == "vector" else "sP", evac_val[idx - 4])
                tv = cx.op("tensor", lambda te, tb=tb, c=c, bnk=bnk: te.transpose(
                    bk(bnk)[:, 0:128], tb[:, c * 128:(c + 1) * 128], ident[:]))
                cx.wait(eng, "tP", tv)
                if eng == "vector":
                    ev = cx.op("vector", lambda v, c=c, t=t, bnk=bnk: v.tensor_copy(
                        out=rowbuf[c][:, t * 128:(t + 1) * 128], in_=bk(bnk)[:, 0:128]))
                else:
                    ev = cx.op("scalar", lambda s, c=c, t=t, bnk=bnk: s.copy(
                        out=rowbuf[c][:, t * 128:(t + 1) * 128], in_=bk(bnk)[:, 0:128]))
                evac_eng.append(eng)
                evac_val.append(ev)
            tp_vals.append(cx.cnt["tP"])
        cx.wait("sync", "vP")
        cx.wait("sync", "sP")
        for c in range(NCH):
            cx.dma("sync", x_fm[c * 128:(c + 1) * 128, :], rowbuf[c])

    def ph1(cx):
        uA, uB = A4(2), A4(3)
        conv_done = []
        for c in range(NCH):
            xb = A4(0) if c % 2 == 0 else A4(1)
            if c >= 2:
                cx.wait("sync", "vP", conv_done[c - 2])
            dv = cx.dma("sync", xb, x_fm[c * 128:(c + 1) * 128, :])
            k0, k1, k2 = ckt[:, c:c + 1], ckt[:, 8 + c:9 + c], ckt[:, 16 + c:17 + c]
            cx.wait("vector", "dS", dv)
            if c >= 1:
                cx.wait("vector", "tP")
            cx.op("vector", lambda v, xb=xb, k0=k0: v.tensor_single_scalar(
                out=uA[:, 1:4096], in_=xb[:, 0:4095], scalar=k0, op=OP.mult))
            cx.op("vector", lambda v, xb=xb, k1=k1: v.scalar_tensor_tensor(
                out=uB[:, 1:4096], in0=xb[:, 1:4096], scalar=k1, in1=uA[:, 1:4096],
                op0=OP.mult, op1=OP.add))
            cx.op("vector", lambda v, xb=xb, k1=k1: v.tensor_single_scalar(
                out=uB[:, 0:1], in_=xb[:, 0:1], scalar=k1, op=OP.mult))
            cx.op("vector", lambda v, xb=xb, k2=k2: v.scalar_tensor_tensor(
                out=uA[:, 0:1], in0=xb[:, 1:2], scalar=k2, in1=uB[:, 0:1],
                op0=OP.mult, op1=OP.add))
            cx.op("vector", lambda v, xb=xb, k2=k2: v.scalar_tensor_tensor(
                out=uA[:, 1:4095], in0=xb[:, 2:4096], scalar=k2, in1=uB[:, 1:4095],
                op0=OP.mult, op1=OP.add))
            vv = cx.op("vector", lambda v: v.tensor_copy(out=uA[:, 4095:4096],
                                                         in_=uB[:, 4095:4096]))
            conv_done.append(vv)
            cx.wait("scalar", "vP", vv)
            sv = cx.op("scalar", lambda s: s.activation(out=uB[:], in_=uA[:], func=AF.Silu))
            if "u_dbg" in _DEV_TAPS:
                cx.wait("sync", "sP", sv)
                du = cx.dma("sync", u_dbg[c * 128:(c + 1) * 128, :], uB[:])
                cx.wait("scalar", "dS", du)
            cx.wait("tensor", "sP", sv)
            for lt in range(NLT):
                cx.op("tensor", lambda te, c=c, lt=lt: te.matmul(
                    PS[0:128, lt * 512:(lt + 1) * 512], wcats[:, c * 128:(c + 1) * 128],
                    uB[:, lt * 512:(lt + 1) * 512],
                    start=(c == 0), stop=(c == NCH - 1)), inc=1 if lt == NLT - 1 else 0)

    def ph2(cx):
        for lt in range(NLT):
            cx.op("scalar", lambda s, lt=lt: s.activation(
                out=D2[0:16, lt * 512:(lt + 1) * 512], in_=PS[0:16, lt * 512:(lt + 1) * 512],
                func=AF.Identity, scale=-1.0, bias=nbbcs[0:16, 0:1]))
            cx.op("scalar", lambda s, lt=lt: s.activation(
                out=D2[32:48, lt * 512:(lt + 1) * 512], in_=PS[32:48, lt * 512:(lt + 1) * 512],
                func=AF.Identity, bias=bbcs[32:48, 0:1]))
            cx.op("scalar", lambda s, lt=lt: s.activation(
                out=D1[0:64, lt * 512:(lt + 1) * 512], in_=PS[64:128, lt * 512:(lt + 1) * 512],
                func=AF.Identity, bias=bbcs[64:128, 0:1]))
        sv0 = cx.cnt["sP"]
        cx.wait("sync", "sP", sv0)
        cx.dma("sync", negBC_d[0:16, :], D2[0:16, :])
        cx.dma("sync", negBC_d[16:32, :], D2[32:48, :])
        sig_val = []
        for c in range(NCH):
            sgb = B4(0) if c % 2 == 0 else B4(1)
            lns = B4(1) if c % 2 == 0 else B4(0)
            if c >= 1:
                cx.wait("scalar", "dS")
            for lt in range(NLT):
                g = c * NLT + lt
                bnk = g % 4
                cx.wait("tensor", "sP", sv0 if g < 4 else sig_val[g - 4])
                tv = cx.op("tensor", lambda te, c=c, lt=lt, bnk=bnk: te.matmul(
                    bk(bnk)[0:128, :], wdtps[:, c * 128:(c + 1) * 128],
                    D1[0:64, lt * 512:(lt + 1) * 512], start=True, stop=True))
                cx.wait("scalar", "tP", tv)
                sg = cx.op("scalar", lambda s, c=c, lt=lt, bnk=bnk, sgb=sgb: s.activation(
                    out=sgb[:, lt * 512:(lt + 1) * 512], in_=bk(bnk),
                    func=AF.Sigmoid, scale=-1.0, bias=nbdtpc[:, c:c + 1]))
                sig_val.append(sg)
            cx.wait("vector", "sP")
            cv = cx.op("vector", lambda v, sgb=sgb: v.tensor_scalar_max(
                out=sgb[:], in0=sgb[:], scalar1=1e-38))
            cx.wait("scalar", "vP", cv)
            s2 = cx.op("scalar", lambda s, sgb=sgb, lns=lns: s.activation(
                out=lns[:], in_=sgb[:], func=AF.Ln))
            cx.wait("sync", "sP", s2)
            cx.dma("sync", lnsig_fm[c * 128:(c + 1) * 128, :], lns)

    def ph3(cx):
        lnsb, xlnsb = A4(0), A4(1)
        alb = [A4(2), A4(3)]
        brep = [B4(0), B4(1)]
        crep = [C4(0), C4(1)]
        bb = [D1, D2]
        scan_vals = []
        ym_vals = []
        br_dma = []
        cr_dma = []
        for c in range(NCH):
            cx.wait("sync", "sP")
            dln = cx.dma("sync", lnsb, lnsig_fm[c * 128:(c + 1) * 128, :])
            cx.wait("sync", "vP")
            dx0 = cx.dma("sync", bb[0][:], x_fm[c * 128:(c + 1) * 128, :])
            cx.wait("vector", "dS", max(dln, dx0))
            cx.op("vector", lambda v: v.tensor_mul(out=xlnsb[:], in0=bb[0][:], in1=lnsb[:]))
            for n in range(N):
                it = c * N + n
                p = it % 2
                if it >= 2:
                    cx.wait("sync", "vP", scan_vals[it - 2])
                dbr = cx.dma("sync", brep[p], negBC_d[n:n + 1, :].partition_broadcast(128))
                br_dma.append(dbr)
                if it >= 2:
                    cx.wait("scalar", "vP", ym_vals[it - 2])
                dcr = cx.dma("scalar", crep[p], negBC_d[N + n:N + n + 1, :].partition_broadcast(128))
                cr_dma.append(dcr)
                cx.wait("scalar", "dS", dln)
                if len(scan_vals) >= 2:
                    cx.wait("scalar", "vP", scan_vals[-2])
                sv = cx.op("scalar", lambda s, c=c, n=n, p=p: s.activation(
                    out=alb[p][:], in_=lnsb[:], func=AF.Exp,
                    scale=aexp[:, c * 16 + n:c * 16 + n + 1]))
                cx.wait("vector", "dS", dbr)
                if it >= 2:
                    cx.wait("vector", "tP", cx.cnt["tP"] - 1)
                cx.op("vector", lambda v, p=p: v.tensor_mul(out=bb[p][:], in0=brep[p][:],
                                                            in1=xlnsb[:]))
                cx.wait("vector", "sP", sv)
                scv = cx.op("vector", lambda v, p=p: v.tensor_tensor_scan(
                    out=D3[:], data0=alb[p][:], data1=bb[p][:], initial=0.0,
                    op0=OP.mult, op1=OP.add))
                scan_vals.append(scv)
                cx.wait("vector", "dA", dcr)
                ymv = cx.op("vector", lambda v, p=p: v.tensor_mul(out=bb[p][:], in0=D3[:],
                                                                  in1=crep[p][:]))
                ym_vals.append(ymv)
                cx.wait("tensor", "vP", ymv)
                for lt in range(NLT):
                    cx.op("tensor", lambda te, lt=lt, n=n, p=p: te.matmul(
                        bk(lt), ident[:], bb[p][:, lt * 512:(lt + 1) * 512],
                        start=(n == 0), stop=(n == N - 1)),
                        inc=1 if lt == NLT - 1 else 0)
            cx.wait("sync", "tP")
            dxr = cx.dma("sync", bb[0][:], x_fm[c * 128:(c + 1) * 128, :])
            cx.wait("vector", "dS", dxr)
            cx.wait("vector", "tP")
            rv = cx.op("vector", lambda v, c=c: v.scalar_tensor_tensor(
                out=xlnsb[:], in0=bb[0][:], scalar=dp1[:, c:c + 1], in1=PS[:, :],
                op0=OP.mult, op1=OP.add))
            cx.wait("sync", "vP", rv)
            cx.dma("sync", resid_fm[c * 128:(c + 1) * 128, :], xlnsb[:])
            cx.wait("tensor", "vP", rv)

    def ph4(cx):
        done = []
        for c in range(NCH):
            rb = A4(0) if c % 2 == 0 else A4(1)
            sq = A4(2) if c % 2 == 0 else A4(3)
            if c >= 2:
                cx.wait("sync", "tP", done[c - 2])
            dv = cx.dma("sync", rb, resid_fm[c * 128:(c + 1) * 128, :])
            cx.wait("scalar", "dS", dv)
            sv = cx.op("scalar", lambda s, rb=rb, sq=sq: s.activation(
                out=sq[:], in_=rb[:], func=AF.Square))
            cx.wait("tensor", "sP", sv)
            for lt in range(NLT):
                cx.op("tensor", lambda te, rb=rb, lt=lt, c=c: te.matmul(
                    PS[0:1, lt * 512:(lt + 1) * 512], ones8[:, 0:1],
                    rb[:, lt * 512:(lt + 1) * 512], start=(c == 0), stop=False), inc=0)
                cx.op("tensor", lambda te, sq=sq, lt=lt, c=c: te.matmul(
                    PS[32:33, lt * 512:(lt + 1) * 512], ones8[:, 0:1],
                    sq[:, lt * 512:(lt + 1) * 512],
                    start=(c == 0), stop=(c == NCH - 1)), inc=1 if lt == NLT - 1 else 0)
            done.append(cx.cnt["tP"])
        cx.wait("vector", "tP")
        cx.op("vector", lambda v: v.tensor_copy(out=B4(0)[0:1, :], in_=PS[0:1, :]))
        vq = cx.op("vector", lambda v: v.tensor_copy(out=D2[32:33, :], in_=PS[32:33, :]))
        cx.wait("sync", "vP", vq)
        dq = cx.dma("sync", B4(1)[0:1, :], D2[32:33, :])
        cx.wait("scalar", "vP", vq)
        cx.wait("scalar", "dS", dq)
        cx.op("scalar", lambda s: s.activation(out=C4(0)[0:1, :], in_=B4(0)[0:1, :],
                                               func=AF.Identity, scale=1.0 / D))
        cx.op("scalar", lambda s: s.activation(out=C4(1)[0:1, :], in_=B4(1)[0:1, :],
                                               func=AF.Identity, scale=1.0 / D))
        cx.op("scalar", lambda s: s.activation(out=B4(0)[0:1, :], in_=C4(0)[0:1, :],
                                               func=AF.Square))
        cx.wait("vector", "sP")
        cx.op("vector", lambda v: v.tensor_sub(out=B4(1)[0:1, :], in0=C4(1)[0:1, :],
                                               in1=B4(0)[0:1, :]))
        cx.wait("scalar", "vP")
        cx.op("scalar", lambda s: s.activation(out=C4(1)[0:1, :], in_=B4(1)[0:1, :],
                                               func=AF.Sqrt, bias=epst[0:1, 0:1]))
        cx.wait("vector", "sP")
        cx.op("vector", lambda v: v.reciprocal(out=D2[0:1, :], in_=C4(1)[0:1, :]))
        vgap(cx)
        cx.op("vector", lambda v: v.tensor_mul(out=B4(0)[0:1, :], in0=C4(0)[0:1, :],
                                               in1=D2[0:1, :]))
        cx.wait("sync", "vP")
        cx.dma("sync", rows_d[0:1, :], B4(0)[0:1, :])
        cx.dma("sync", rows_d[1:2, :], D2[0:1, :])

    def ph5(cx):
        cx.dma("sync", B4(0), rows_d[1:2, :].partition_broadcast(128))
        d1 = cx.dma("sync", B4(1), rows_d[0:1, :].partition_broadcast(128))
        mul_vals = []
        for c in range(NCH):
            rb = A4(0) if c % 2 == 0 else A4(1)
            xo = A4(2) if c % 2 == 0 else A4(3)
            if c >= 2:
                cx.wait("sync", "vP", mul_vals[c - 2] + 1)
            dv = cx.dma("sync", rb, resid_fm[c * 128:(c + 1) * 128, :])
            cx.wait("vector", "dS", dv)
            mv = cx.op("vector", lambda v, rb=rb: v.tensor_mul(out=D1[:], in0=rb[:], in1=B4(0)))
            mul_vals.append(mv)
            vv = cx.op("vector", lambda v, xo=xo: v.tensor_sub(out=xo[:], in0=D1[:], in1=B4(1)))
            cx.wait("sync", "vP", vv)
            cx.dma("sync", xln1_fm[c * 128:(c + 1) * 128, :], xo)

    def ph6(cx):
        ev_hist = []
        tp_hist = []
        wr_hist = []
        for t in range(32):
            tin = D2 if t % 2 == 0 else D3
            tout = BIGB[:, 0:1024] if t % 2 == 0 else BIGB[:, 1024:2048]
            if t >= 2:
                cx.wait("sync", "tP", tp_hist[t - 2])
            src = xln1_fm.rearrange("(c p) l -> p c l", p=128)[:, :, t * 128:(t + 1) * 128]
            dv = cx.dma("sync", tin[:, 0:1024].rearrange("p (c l) -> p c l", c=8), src)
            for c in range(NCH):
                idx = t * 8 + c
                bnk = idx % 2
                cx.wait("tensor", "dS", dv)
                if idx >= 2:
                    cx.wait("tensor", "vP", ev_hist[idx - 2])
                tv = cx.op("tensor", lambda te, tin=tin, c=c, bnk=bnk: te.transpose(
                    bk(bnk)[:, 0:128], tin[:, c * 128:(c + 1) * 128], ident[:]))
                cx.wait("vector", "tP", tv)
                if t >= 2 and c == 0:
                    cx.wait("vector", "dS", wr_hist[t - 2])
                ev = cx.op("vector", lambda v, tout=tout, c=c, bnk=bnk: v.tensor_copy(
                    out=tout[:, c * 128:(c + 1) * 128], in_=bk(bnk)[:, 0:128]))
                ev_hist.append(ev)
            tp_hist.append(cx.cnt["tP"])
            cx.wait("sync", "vP", ev_hist[-1])
            wr = cx.dma("sync", xln1_tm[t * 128:(t + 1) * 128, :], tout)
            wr_hist.append(wr)

    def ph7(cx):
        don = cx.dma("sync", B4(0)[0:8, 0:2048], c_onerow[0:1, 0:2048].partition_broadcast(8))
        for tt in range(NTT):
            g = BIGA[:, tt * 1024:(tt + 1) * 1024]
            dg = cx.idma(g, None, xln1_tm[:, :], IOffs(toks[:, tt:tt + 1], 0))
            for c in range(NCH):
                idx = tt * 8 + c
                cx.wait("tensor", "dG", dg)
                if idx >= 2:
                    cx.wait("tensor", "vP")
                tv = cx.op("tensor", lambda te, g=g, c=c, idx=idx: te.transpose(
                    bk(idx % 2)[:, 0:128], g[:, c * 128:(c + 1) * 128], ident[:]))
                cx.wait("vector", "tP", tv)
                cx.op("vector", lambda v, c=c, idx=idx: v.tensor_copy(
                    out=D1[:, 2048 + c * 128:2048 + (c + 1) * 128], in_=bk(idx % 2)[:, 0:128]))
            cx.wait("tensor", "vP")
            cx.wait("tensor", "sP")
            for c in range(NCH):
                cx.op("tensor", lambda te, c=c: te.matmul(
                    PS[0:8, 1024:1152], cents[:, c * 8:(c + 1) * 8],
                    D1[:, 2048 + c * 128:2048 + (c + 1) * 128],
                    start=(c == 0), stop=(c == NCH - 1)), inc=1 if c == NCH - 1 else 0)
            cx.wait("scalar", "tP")
            se = cx.op("scalar", lambda s: s.copy(out=D2[0:8, 1024:1152], in_=PS[0:8, 1024:1152]))
            cx.wait("tensor", "sP", se)
            cx.wait("tensor", "vP")
            tv2 = cx.op("tensor", lambda te: te.transpose(
                PS[:, 1536:1544], D2[0:8, 1024:1152], ident[0:8, 0:8]))
            cx.wait("vector", "tP", tv2)
            cx.op("vector", lambda v, tt=tt: v.tensor_copy(
                out=D2[:, tt * 8:(tt + 1) * 8], in_=PS[:, 1536:1544]))
            vv = cx.op("vector", lambda v, tt=tt: v.reduce_max(
                out=rowmaxs[:, tt:tt + 1], in_=PS[:, 1536:1544], axis=AX.X))
            cx.wait("scalar", "vP", vv)
            sb = cx.op("scalar", lambda s, tt=tt: s.copy(
                out=smallA[:, 32 + tt % 2:33 + tt % 2], in_=rowmaxs[:, tt:tt + 1]))
            cx.op("scalar", lambda s, tt=tt: s.activation(
                out=alphaff[:, tt:tt + 1], in_=rowmaxs[:, tt:tt + 1], func=AF.Sigmoid), inc=0)
            cx.wait("vector", "sP", sb)
            cx.op("vector", lambda v, tt=tt: v.tensor_single_scalar(
                out=smallA[:, 40:48], in_=D2[:, tt * 8:(tt + 1) * 8],
                scalar=smallA[:, 32 + tt % 2:33 + tt % 2], op=OP.is_equal))
            vgap(cx)
            cx.op("vector", lambda v: v.tensor_tensor_scan(
                out=smallA[:, 48:56], data0=ones8[:], data1=smallA[:, 40:48],
                initial=0.0, op0=OP.mult, op1=OP.add))
            vgap(cx)
            cx.op("vector", lambda v: v.tensor_single_scalar(
                out=smallA[:, 56:64], in_=smallA[:, 48:56], scalar=1.0, op=OP.is_equal))
            vgap(cx)
            vv2 = cx.op("vector", lambda v: v.tensor_mul(
                out=smallA[:, 64:72], in0=smallA[:, 40:48], in1=smallA[:, 56:64]))
            if "oh_dbg" in _DEV_TAPS and tt == 0:
                cx.wait("sync", "vP", vv2)
                cx.dma("sync", oh_dbg[:, :], smallA[:, 0:96])
                cx.dma("sync", aff_dbg[:, 0:8], D2[:, 0:8])
                cx.dma("sync", aff_dbg[:, 8:10], rowmaxs[:, 0:2])
                cx.wait("vector", "dS")
            cx.wait("tensor", "vP", vv2)
            tv3 = cx.op("tensor", lambda te: te.transpose(
                PS[0:8, 2048:2176], smallA[:, 64:72], ident[:]))
            cx.wait("vector", "tP", tv3)
            cx.op("vector", lambda v, tt=tt: v.tensor_copy(
                out=D3[0:8, tt * 128:(tt + 1) * 128], in_=PS[0:8, 2048:2176]))
        cx.wait("vector", "dS", don)
        cx.op("vector", lambda v: v.tensor_tensor_scan(
            out=B4(0)[0:8, 2048:4096], data0=B4(0)[0:8, 0:2048], data1=D3[0:8, 0:2048],
            initial=0.0, op0=OP.mult, op1=OP.add))
        vgap(cx)
        vv3 = cx.op("vector", lambda v: v.tensor_mul(
            out=B4(1)[0:8, 0:2048], in0=D3[0:8, 0:2048], in1=B4(0)[0:8, 2048:4096]))
        cx.wait("tensor", "vP", vv3)
        for q in range(4):
            cx.op("tensor", lambda te, q=q: te.matmul(
                PS[0:1, q * 512:(q + 1) * 512], ones8[0:8, 0:1],
                B4(1)[0:8, q * 512:(q + 1) * 512], start=True, stop=True), inc=0)
            cx.op("tensor", lambda te, q=q: te.matmul(
                PS[0:1, 2048 + q * 512:2048 + (q + 1) * 512], iotaEs[:],
                D3[0:8, q * 512:(q + 1) * 512], start=True, stop=True),
                inc=1 if q == 3 else 0)
        cx.wait("vector", "tP")
        cx.op("vector", lambda v: v.tensor_copy(out=C4(0)[0:1, 0:2048], in_=PS[0:1, 0:2048]))
        cx.op("vector", lambda v: v.tensor_copy(out=C4(0)[0:1, 2048:4096], in_=PS[0:1, 2048:4096]))
        rnk = C4(0)[0:1, 0:2048]
        eidr = C4(0)[0:1, 2048:4096]
        vgap(cx)
        cx.op("vector", lambda v: v.tensor_single_scalar(
            out=C4(1)[0:1, 0:2048], in_=eidr, scalar=float(CAP), op=OP.mult))
        vgap(cx)
        cx.op("vector", lambda v: v.tensor_add(
            out=C4(1)[0:1, 2048:4096], in0=C4(1)[0:1, 0:2048], in1=rnk))
        vgap(cx)
        cx.op("vector", lambda v: v.tensor_single_scalar(
            out=B4(1)[0:1, 2048:4096], in_=C4(1)[0:1, 2048:4096], scalar=1.0, op=OP.subtract))
        cx.op("vector", lambda v: v.tensor_single_scalar(
            out=C4(1)[0:1, 0:2048], in_=rnk, scalar=float(CAP) + 0.5, op=OP.is_ge))
        vgap(cx)
        cx.op("vector", lambda v: v.tensor_single_scalar(
            out=C4(1)[0:1, 2048:4096], in_=B4(1)[0:1, 2048:4096], scalar=float(NSLOT),
            op=OP.subtract))
        vgap(cx)
        cx.op("vector", lambda v: v.tensor_mul(
            out=B4(0)[0:1, 0:2048], in0=C4(1)[0:1, 0:2048], in1=C4(1)[0:1, 2048:4096]))
        vgap(cx)
        vslot = cx.op("vector", lambda v: v.tensor_sub(
            out=C4(0)[0:1, 2048:4096], in0=B4(1)[0:1, 2048:4096], in1=B4(0)[0:1, 0:2048]))
        slotF = C4(0)[0:1, 2048:4096]
        cx.wait("tensor", "vP", vslot)
        for tt in range(NTT):
            cx.wait("tensor", "vP")
            tv4 = cx.op("tensor", lambda te, tt=tt, slotF=slotF: te.transpose(
                PS[:, 2560:2561], slotF[0:1, tt * 128:(tt + 1) * 128], ident[0:1, 0:1]))
            cx.wait("vector", "tP", tv4)
            cx.op("vector", lambda v, tt=tt: v.tensor_copy(
                out=slotf[:, tt:tt + 1], in_=PS[:, 2560:2561]))
        vgap(cx)
        cx.op("vector", lambda v: v.tensor_copy(out=slot32[:], in_=slotf[:]))
        if "aff_dbg" in _DEV_TAPS:
            cx.wait("sync", "vP")
            cx.dma("sync", aff_dbg[:, :], D2[:, 0:128])
        if "route_dbg" in _DEV_TAPS:
            cx.wait("sync", "vP")
            cx.dma("sync", route_dbg[0:1, :], rnk)
            cx.dma("sync", route_dbg[1:2, :], eidr)
            cx.dma("sync", route_dbg[2:3, :], slotF)

    def ph8(cx):
        for tt in range(NTT):
            g = BIGA[:, tt * 1024:(tt + 1) * 1024]
            cx.wait("vector", "dG")
            cx.op("vector", lambda v, g=g: v.reduce_sum(
                out=smallA[:, 0:1], in_=g[:], axis=AX.X))
            cx.wait("scalar", "vP")
            cx.op("scalar", lambda s, g=g: s.activation(
                out=D1[:, 0:1024], in_=g[:], func=AF.Square))
            cx.wait("vector", "sP")
            cx.op("vector", lambda v: v.reduce_sum(
                out=smallA[:, 1:2], in_=D1[:, 0:1024], axis=AX.X))
            cx.wait("scalar", "vP")
            cx.op("scalar", lambda s: s.activation(out=smallA[:, 2:3], in_=smallA[:, 0:1],
                                                   func=AF.Copy, scale=1.0 / D))
            cx.op("scalar", lambda s: s.activation(out=smallA[:, 3:4], in_=smallA[:, 1:2],
                                                   func=AF.Copy, scale=1.0 / D))
            cx.op("scalar", lambda s: s.activation(out=smallA[:, 4:5], in_=smallA[:, 2:3],
                                                   func=AF.Square))
            cx.wait("vector", "sP")
            cx.op("vector", lambda v: v.tensor_sub(out=smallA[:, 5:6], in0=smallA[:, 3:4],
                                                   in1=smallA[:, 4:5]))
            cx.wait("scalar", "vP")
            cx.op("scalar", lambda s: s.activation(out=smallA[:, 6:7], in_=smallA[:, 5:6],
                                                   func=AF.Sqrt, bias=epst[:, 0:1]))
            cx.wait("vector", "sP")
            vv = cx.op("vector", lambda v: v.reciprocal(out=smallA[:, 7:8], in_=smallA[:, 6:7]))
            cx.wait("scalar", "vP", vv)
            cx.op("scalar", lambda s: s.copy(out=smallA[:, 8:9], in_=smallA[:, 2:3]))
            sb = cx.op("scalar", lambda s: s.copy(out=smallA[:, 9:10], in_=smallA[:, 7:8]))
            cx.wait("vector", "sP", sb)
            vg = cx.op("vector", lambda v, g=g: v.tensor_scalar(
                out=D1[:, 1024:2048], in0=g[:], scalar1=smallA[:, 8:9],
                scalar2=smallA[:, 9:10], op0=OP.subtract, op1=OP.mult))
            cx.wait("tensor", "vP", vg)
            for hh in range(2):
                cx.op("tensor", lambda te, tt=tt, hh=hh: te.matmul(
                    bk(hh), D3[0:8, tt * 128:(tt + 1) * 128],
                    lngs[:, hh * 512:(hh + 1) * 512], start=True, stop=True), inc=0)
                cx.op("tensor", lambda te, tt=tt, hh=hh: te.matmul(
                    bk(2 + hh), D3[0:8, tt * 128:(tt + 1) * 128],
                    lnbs[:, hh * 512:(hh + 1) * 512], start=True, stop=True),
                    inc=1 if hh == 1 else 0)
            cx.wait("vector", "tP")
            for hh in range(2):
                cx.op("vector", lambda v, hh=hh: v.tensor_mul(
                    out=D1[:, 2048 + hh * 512:2048 + (hh + 1) * 512],
                    in0=D1[:, 1024 + hh * 512:1024 + (hh + 1) * 512], in1=bk(hh)))
            for hh in range(2):
                cx.op("vector", lambda v, hh=hh: v.tensor_add(
                    out=D1[:, 3072 + hh * 512:3072 + (hh + 1) * 512],
                    in0=D1[:, 2048 + hh * 512:2048 + (hh + 1) * 512], in1=bk(2 + hh)))
            vv2 = cx.cnt["vP"]
            cx.wait("gpsimd", "vP", vv2)
            cx.idma(slotx[:, :], IOffs(slot32[:, tt:tt + 1], 0), D1[:, 3072:4096], None)

    def ph9(cx):
        NST = CAP // 128
        BIGAb = BIGA.bitcast(BF16)
        BIGCb = BIGC.bitcast(BF16)
        D2b = D2.bitcast(BF16)

        def whi(p):
            return BIGCb[:, p * 8192:p * 8192 + 4096]

        def wlo(p):
            return BIGCb[:, p * 8192 + 4096:p * 8192 + 8192]

        h_hi = BIGAb[:, 0:16384]
        h_lo = BIGAb[:, 16384:32768]
        xs_hi = D2b[:, 0:4096]
        xs_lo = D2b[:, 4096:8192]
        wg = []
        for e in range(E):
            cx.wait("gpsimd", "tP")
            cx.wait("gpsimd", "vP")
            dgl = 0
            for st in range(NST):
                dgl = cx.dma("gpsimd", D3[:, st * 1024:(st + 1) * 1024],
                             slotx[e * CAP + st * 128:e * CAP + (st + 1) * 128, :])
            for st in range(NST):
                for c in range(NCH):
                    idx = st * 8 + c
                    cx.wait("tensor", "dG", dgl)
                    if idx >= 2:
                        cx.wait("tensor", "vP")
                    tv = cx.op("tensor", lambda te, st=st, c=c, idx=idx: te.transpose(
                        bk(idx % 2)[:, 0:128],
                        D3[:, st * 1024 + c * 128:st * 1024 + (c + 1) * 128], ident[:]))
                    cx.wait("vector", "tP", tv)
                    cx.op("vector", lambda v, st=st, c=c, idx=idx: v.tensor_copy(
                        out=D1[:, c * CAP + st * 128:c * CAP + (st + 1) * 128],
                        in_=bk(idx % 2)[:, 0:128]))
            cx.op("vector", lambda v: v.tensor_copy(out=xs_hi[:, :], in_=D1[:]))
            vxs = cx.op("vector", lambda v: v.tensor_tensor(
                out=xs_lo[:, :], in0=D1[:], in1=xs_hi[:, :], op=OP.subtract))
            for q in range(8):
                p = q % 2
                if len(wg) >= 2:
                    cx.wait("sync", "tP", wg[-2])
                dw = cx.dma("sync", B4(p).rearrange("p (c f) -> p c f", c=8),
                            w1[e, :, q * 512:(q + 1) * 512].rearrange("(c p) f -> p c f", p=128))
                cx.wait("vector", "dS", dw)
                cx.op("vector", lambda v, p=p: v.tensor_copy(out=whi(p)[:, :], in_=B4(p)))
                wsp = cx.op("vector", lambda v, p=p: v.tensor_tensor(
                    out=wlo(p)[:, :], in0=B4(p), in1=whi(p)[:, :], op=OP.subtract))
                cx.wait("tensor", "vP", max(vxs, wsp))
                for fc in range(4):
                    fi = q * 4 + fc
                    hb = 2 + fi % 2
                    if fi >= 2:
                        cx.wait("tensor", "vP")
                    for c in range(NCH):
                        for ph_, lhs, rhs in ((0, whi(p), xs_hi), (1, whi(p), xs_lo),
                                              (2, wlo(p), xs_hi)):
                            cx.op("tensor", lambda te, lhs=lhs, rhs=rhs, c=c, fc=fc, hb=hb,
                                  ph_=ph_: te.matmul(
                                      bk(hb), lhs[:, c * 512 + fc * 128:c * 512 + (fc + 1) * 128],
                                      rhs[:, c * 512:(c + 1) * 512],
                                      start=(c == 0 and ph_ == 0),
                                      stop=(c == NCH - 1 and ph_ == 2)),
                                  inc=1 if (c == NCH - 1 and ph_ == 2) else 0)
                    cx.wait("scalar", "tP")
                    htmp = D1[:, (fi % 2) * 512:(fi % 2) * 512 + 512]
                    if fi >= 2:
                        cx.wait("scalar", "vP")
                    cx.op("scalar", lambda s, e=e, fi=fi, hb=hb, htmp=htmp: s.activation(
                        out=htmp, in_=bk(hb),
                        func=AF.Relu, bias=b1c[:, e * 32 + fi:e * 32 + fi + 1]))
                    sh = cx.op("scalar", lambda s, fi=fi, htmp=htmp: s.copy(
                        out=h_hi[:, fi * 512:(fi + 1) * 512], in_=htmp))
                    cx.wait("vector", "sP", sh)
                    cx.op("vector", lambda v, fi=fi, htmp=htmp: v.tensor_tensor(
                        out=h_lo[:, fi * 512:(fi + 1) * 512], in0=htmp,
                        in1=h_hi[:, fi * 512:(fi + 1) * 512], op=OP.subtract))
                wg.append(cx.cnt["tP"])
            smm1 = cx.cnt["vP"]
            if "h_dbg" in _DEV_TAPS and e == 0:
                cx.wait("sync", "vP", smm1)
                cx.wait("sync", "sP")
                cx.dma("sync", h_dbg[:, 0:8192], BIGA[:, 0:8192])
                cx.dma("sync", h_dbg[:, 8192:16384], BIGA[:, 8192:16384])
                cx.wait("tensor", "dS")
            for fi in range(32):
                q, fc = fi // 4, fi % 4
                if fc == 0:
                    p = q % 2
                    if len(wg) >= 2:
                        cx.wait("sync", "tP", wg[-2])
                    dw2 = cx.dma("sync", B4(p).rearrange("p (fc d) -> p fc d", fc=4),
                                 w2[e, q * 512:(q + 1) * 512, :]
                                 .rearrange("(fc p) d -> p fc d", p=128))
                    cx.wait("vector", "dS", dw2)
                    cx.op("vector", lambda v, p=p: v.tensor_copy(out=whi(p)[:, :], in_=B4(p)))
                    wsp2 = cx.op("vector", lambda v, p=p: v.tensor_tensor(
                        out=wlo(p)[:, :], in0=B4(p), in1=whi(p)[:, :], op=OP.subtract))
                    cx.wait("tensor", "vP", max(smm1, wsp2))
                for do in range(8):
                    for ph_, lhs, rhsof in ((0, whi(p), 0), (1, whi(p), 16384),
                                            (2, wlo(p), 0)):
                        cx.op("tensor", lambda te, lhs=lhs, fc=fc, do=do, fi=fi, ph_=ph_,
                              rhsof=rhsof: te.matmul(
                                  bk(do), lhs[:, fc * 1024 + do * 128:fc * 1024 + (do + 1) * 128],
                                  BIGAb[:, rhsof + fi * 512:rhsof + (fi + 1) * 512],
                                  start=(fi == 0 and ph_ == 0),
                                  stop=(fi == 31 and ph_ == 2)),
                              inc=1 if (do == 7 and ph_ == 2) else 0)
                if fc == 3:
                    wg.append(cx.cnt["tP"])
            cx.wait("scalar", "tP")
            for do in range(8):
                cx.op("scalar", lambda s, e=e, do=do: s.activation(
                    out=D1[:, do * 512:(do + 1) * 512], in_=bk(do),
                    func=AF.Identity, bias=b2c[:, e * 8 + do:e * 8 + do + 1]))
            sy2 = cx.cnt["sP"]
            for st in range(NST):
                for do in range(8):
                    idx = st * 8 + do
                    cx.wait("tensor", "sP", sy2)
                    if idx >= 2:
                        cx.wait("tensor", "vP")
                    t2 = cx.op("tensor", lambda te, st=st, do=do, idx=idx: te.transpose(
                        bk(idx % 2)[:, 0:128],
                        D1[:, do * 512 + st * 128:do * 512 + (st + 1) * 128], ident[:]))
                    cx.wait("vector", "tP", t2)
                    if e >= 1 and idx == 0:
                        cx.wait("vector", "dG")
                    cx.op("vector", lambda v, st=st, do=do, idx=idx: v.tensor_copy(
                        out=D3[:, st * 1024 + do * 128:st * 1024 + (do + 1) * 128],
                        in_=bk(idx % 2)[:, 0:128]))
            vy2 = cx.cnt["vP"]
            cx.wait("gpsimd", "vP", vy2)
            for st in range(NST):
                cx.dma("gpsimd", sloty[e * CAP + st * 128:e * CAP + (st + 1) * 128, :],
                       D3[:, st * 1024:(st + 1) * 1024])

    def ph10(cx):
        stt_vals = []
        out_wr = []
        for tt in range(NTT):
            fb = D1 if tt % 2 == 0 else D2
            if tt >= 2:
                cx.wait("gpsimd", "vP", stt_vals[tt - 2])
            dg1 = cx.idma(fb[:, 0:1024], None, sloty[:, :], IOffs(slot32[:, tt:tt + 1], 0))
            dg2 = cx.idma(fb[:, 1024:2048], None, xln1_tm[:, :], IOffs(toks[:, tt:tt + 1], 0))
            cx.wait("vector", "dG", dg2)
            cx.op("vector", lambda v, fb=fb, tt=tt: v.tensor_single_scalar(
                out=D3[:, 0:1024], in_=fb[:, 0:1024], scalar=alphaff[:, tt:tt + 1],
                op=OP.mult))
            vgap(cx)
            sv0 = cx.op("vector", lambda v, fb=fb: v.scalar_tensor_tensor(
                out=D3[:, 1024:2048], in0=fb[:, 1024:2048], scalar=2.0, in1=D3[:, 0:1024],
                op0=OP.mult, op1=OP.add))
            stt_vals.append(sv0)
            vgap(cx)
            cx.op("vector", lambda v: v.reduce_sum(out=smallA[:, 16:17],
                                                   in_=D3[:, 1024:2048], axis=AX.X))
            cx.wait("scalar", "vP")
            cx.op("scalar", lambda s: s.activation(out=D3[:, 0:1024], in_=D3[:, 1024:2048],
                                                   func=AF.Square))
            cx.wait("vector", "sP")
            cx.op("vector", lambda v: v.reduce_sum(out=smallA[:, 17:18],
                                                   in_=D3[:, 0:1024], axis=AX.X))
            cx.wait("scalar", "vP")
            cx.op("scalar", lambda s: s.activation(out=smallA[:, 18:19], in_=smallA[:, 16:17],
                                                   func=AF.Copy, scale=1.0 / D))
            cx.op("scalar", lambda s: s.activation(out=smallA[:, 19:20], in_=smallA[:, 17:18],
                                                   func=AF.Copy, scale=1.0 / D))
            cx.op("scalar", lambda s: s.activation(out=smallA[:, 20:21], in_=smallA[:, 18:19],
                                                   func=AF.Square))
            cx.wait("vector", "sP")
            cx.op("vector", lambda v: v.tensor_sub(out=smallA[:, 21:22], in0=smallA[:, 19:20],
                                                   in1=smallA[:, 20:21]))
            cx.wait("scalar", "vP")
            cx.op("scalar", lambda s: s.activation(out=smallA[:, 22:23], in_=smallA[:, 21:22],
                                                   func=AF.Sqrt, bias=epst[:, 0:1]))
            cx.wait("vector", "sP")
            vr = cx.op("vector", lambda v: v.reciprocal(out=smallA[:, 23:24],
                                                        in_=smallA[:, 22:23]))
            cx.wait("scalar", "vP", vr)
            cx.op("scalar", lambda s: s.copy(out=smallA[:, 24:25], in_=smallA[:, 18:19]))
            sb = cx.op("scalar", lambda s: s.copy(out=smallA[:, 25:26], in_=smallA[:, 23:24]))
            cx.wait("vector", "sP", sb)
            ob = D3[:, 2048:3072] if tt % 2 == 0 else D3[:, 3072:4096]
            if tt >= 2:
                cx.wait("vector", "dS", out_wr[tt - 2])
            vv = cx.op("vector", lambda v, ob=ob: v.tensor_scalar(
                out=ob, in0=D3[:, 1024:2048], scalar1=smallA[:, 24:25],
                scalar2=smallA[:, 25:26], op0=OP.subtract, op1=OP.mult))
            cx.wait("sync", "vP", vv)
            wr = cx.dma("sync", out_rows[tt * 128:(tt + 1) * 128, :], ob)
            out_wr.append(wr)

    phases = [ph0, ph1, ph2, ph3, ph4, ph5, ph6, ph7, ph8, ph9, ph10]
    phases = phases[:_LAST_PHASE + 1]

    def engine_program(name):
        def run(eng):
            cx.begin(name, eng)
            for ph in phases:
                ph(cx)
                cx.barrier()
        return run

    with nc.Block() as block:
        block.sync(engine_program("sync"))
        block.scalar(engine_program("scalar"))
        block.vector(engine_program("vector"))
        block.tensor(engine_program("tensor"))
        block.gpsimd(engine_program("gpsimd"))

    return nc


_CACHE = {}


def _get_nc():
    key = (_LAST_PHASE, tuple(_DEV_TAPS))
    if key not in _CACHE:
        _CACHE[key] = _build()
    return _CACHE[key]


def _in_maps(inputs):
    xfull = np.ascontiguousarray(inputs["L0_attn_x"], dtype=np.float32)
    consts = {
        "c_ident": np.eye(128, dtype=np.float32),
        "c_ones": np.ones((128, 8), np.float32),
        "c_iotaE": np.arange(8, dtype=np.float32).reshape(8, 1),
        "c_onerow": np.ones((1, 4096), np.float32),
        "c_zrow": np.zeros((1, 4096), np.float32),
        "c_eps": np.full((128, 1), 1e-5, np.float32),
    }
    wmap = {
        "A_log": "L0_attn_A_log", "W_dtp": "L0_attn_W_dtp",
        "ln_g": "L0_ffn_ln_g", "ln_b": "L0_ffn_ln_b", "w1": "L0_ffn_w1",
        "w2": "L0_ffn_w2",
    }
    weights = {k: np.ascontiguousarray(inputs[v], dtype=np.float32) for k, v in wmap.items()}
    f32c = lambda a: np.ascontiguousarray(a, dtype=np.float32)
    weights["ck_l"] = f32c(np.asarray(inputs["L0_attn_conv_kernel"], np.float32)
                           .reshape(3, 8, 128).transpose(2, 0, 1).reshape(128, 24))
    _wc = np.zeros((1024, 128), np.float32)
    _wbc_np = np.asarray(inputs["L0_attn_W_bc"], np.float32)
    _wc[:, 0:16] = _wbc_np[:, 0:16]
    _wc[:, 32:48] = _wbc_np[:, 16:32]
    _wc[:, 64:128] = np.asarray(inputs["L0_attn_W_dt"], np.float32)
    weights["wcat_l"] = f32c(_wc.reshape(8, 128, 128).transpose(1, 0, 2).reshape(128, 1024))
    _bl = np.zeros((128, 1), np.float32)
    _bl[0:16, 0] = np.asarray(inputs["L0_attn_b_bc"], np.float32)[0:16]
    _bl[32:48, 0] = np.asarray(inputs["L0_attn_b_bc"], np.float32)[16:32]
    _bl[64:128, 0] = np.asarray(inputs["L0_attn_b_dt"], np.float32)
    weights["b_bc_l"] = _bl
    weights["b_dtp_l"] = f32c(np.asarray(inputs["L0_attn_b_dtp"], np.float32).reshape(8, 128).T)
    weights["Dco_l"] = f32c(np.asarray(inputs["L0_attn_Dcoeff"], np.float32).reshape(8, 128).T)
    weights["cent_l"] = f32c(np.asarray(inputs["L0_ffn_centroids"], np.float32)
                             .reshape(8, 8, 128).transpose(2, 1, 0).reshape(128, 64))
    weights["b1_l"] = f32c(np.asarray(inputs["L0_ffn_b1"], np.float32)
                           .reshape(8, 32, 128).transpose(2, 0, 1).reshape(128, 256))
    weights["b2_l"] = f32c(np.asarray(inputs["L0_ffn_b2"], np.float32)
                           .reshape(8, 8, 128).transpose(2, 0, 1).reshape(128, 64))
    maps = []
    for j in range(8):
        b, h = j // 2, j % 2
        m = dict(consts)
        m.update(weights)
        m["x"] = xfull[b]
        m["tok_l"] = np.ascontiguousarray(
            (h * TPC + np.arange(TPC)).astype(np.int32).reshape(16, 128).T)
        maps.append(m)
    return maps


def kernel(**inputs):
    nc = _get_nc()
    res = run_bass_kernel_spmd(nc, _in_maps(inputs), core_ids=list(range(8)))
    out = np.empty((B, L, D), np.float32)
    for j in range(8):
        b, h = j // 2, j % 2
        out[b, h * TPC:(h + 1) * TPC] = res.results[j]["out_rows"]
    return out


# version 05 (final)
# speedup vs baseline: 1.1172x; 1.0892x over previous
import numpy as np
import concourse.bass as bass
import concourse.mybir as mybir
from concourse.bass_utils import run_bass_kernel_spmd

F32 = mybir.dt.float32
BF16 = mybir.dt.bfloat16
I32 = mybir.dt.int32
AF = mybir.ActivationFunctionType
OP = mybir.AluOpType
AX = mybir.AxisListType
IOffs = bass.IndirectOffsetOnAxis

B, L, D, N = 4, 4096, 1024, 16
DT_RANK = 64
E, FF, CAP = 8, 4096, 384
NSLOT = E * CAP
TPC = 2048
NTT = TPC // 128
NCH = D // 128
NLT = L // 512

_DEV_TAPS = ()
_LAST_PHASE = 10

_ENGINES = ("sync", "scalar", "vector", "tensor", "gpsimd")


class Ctx:

    def __init__(self, nc, sems):
        self.nc = nc
        self.sems = sems
        self.active = None
        self.eng = None
        self.cnt = {k: 0 for k in sems}
        self.queue_sem = {"sync": "dS", "scalar": "dA", "gpsimd": "dG"}

    def begin(self, engine, eng_handle):
        self.active = engine
        self.eng = eng_handle
        self.cnt = {k: 0 for k in self.sems}

    def dma(self, queue, out, in_, **kw):
        s = self.queue_sem[queue]
        self.cnt[s] += 16
        if self.active == queue:
            self.eng.dma_start(out=out, in_=in_, **kw).then_inc(self.sems[s], 16)
        return self.cnt[s]

    def idma(self, out, out_off, in_, in_off):
        self.cnt["dG"] += 16
        if self.active == "gpsimd":
            self.eng.indirect_dma_start(out=out, out_offset=out_off, in_=in_,
                                        in_offset=in_off).then_inc(self.sems["dG"], 16)
        return self.cnt["dG"]

    def op(self, engine, fn, inc=1):
        sem = {"scalar": "sP", "vector": "vP", "tensor": "tP", "gpsimd": "gP"}[engine]
        self.cnt[sem] += inc
        if self.active == engine:
            instr = fn(self.eng)
            if inc:
                if engine == "tensor":
                    self.eng.drain().then_inc(self.sems[sem], inc)
                else:
                    instr.then_inc(self.sems[sem], inc)
        return self.cnt[sem]

    def wait(self, engine, sem, val=None):
        v = self.cnt[sem] if val is None else val
        if self.active == engine and v > 0:
            self.eng.wait_ge(self.sems[sem], v)

    def barrier(self):
        for s in ("dS", "dA", "dG", "vP", "sP", "tP", "gP"):
            for e in _ENGINES:
                self.wait(e, s)
        self.cnt["bar"] += 5
        if self.active is not None:
            self.eng.sem_inc(self.sems["bar"], 1)
            self.eng.wait_ge(self.sems["bar"], self.cnt["bar"])


def _build():
    nc = bass.Bass("TRN2", target_bir_lowering=False)

    def din(name, shape, dt=F32):
        return nc.dram_tensor(name, shape, dt, kind="ExternalInput").ap()

    def dscratch(name, shape, dt=F32):
        kind = "ExternalOutput" if name in _DEV_TAPS else "Internal"
        return nc.dram_tensor(name, shape, dt, kind=kind).ap()

    x_in = din("x", [L, D])
    tokl = din("tok_l", [128, 16], I32)
    ckl = din("ck_l", [128, 24])
    alog = din("A_log", [D, N])
    wcatl = din("wcat_l", [128, 1024])
    bbcl = din("b_bc_l", [128, 1])
    dcol_in = din("Dco_l", [128, 8])


    wdtp = din("W_dtp", [DT_RANK, D])
    bdtpl = din("b_dtp_l", [128, 8])
    centl = din("cent_l", [128, 64])
    lng = din("ln_g", [E, D])
    lnb = din("ln_b", [E, D])
    w1 = din("w1", [E, D, FF])
    b1l = din("b1_l", [128, 256])
    w2 = din("w2", [E, FF, D])
    b2l = din("b2_l", [128, 64])
    c_ident = din("c_ident", [128, 128])
    c_ones = din("c_ones", [128, 8])
    c_iotaE = din("c_iotaE", [8, 1])
    c_onerow = din("c_onerow", [1, 4096])
    c_zrow = din("c_zrow", [1, 4096])
    c_eps = din("c_eps", [128, 1])

    out_rows = nc.dram_tensor("out_rows", [TPC, D], F32, kind="ExternalOutput").ap()

    x_fm = dscratch("x_fm", [D, L])
    lnsig_fm = dscratch("lnsig_fm", [D, L])
    resid_fm = dscratch("resid_fm", [D, L])
    xln1_fm = dscratch("xln1_fm", [D, L])
    xln1_tm = dscratch("xln1_tm", [L, D])
    negBC_d = dscratch("negBC_d", [2 * N, L])
    rows_d = dscratch("rows_d", [2, L])
    u_dbg = dscratch("u_dbg", [D, L])
    route_dbg = dscratch("route_dbg", [4, TPC])
    aff_dbg = dscratch("aff_dbg", [128, 128])
    oh_dbg = dscratch("oh_dbg", [128, 96])
    h_dbg = dscratch("h_dbg", [128, 16384])
    slotx = dscratch("slotx", [NSLOT + 128, D])
    sloty = dscratch("sloty", [NSLOT + 128, D])

    BIGA = nc.alloc_sbuf_tensor("BIGA", [128, 16384], F32).ap()
    BIGB = nc.alloc_sbuf_tensor("BIGB", [128, 8192], F32).ap()
    BIGC = nc.alloc_sbuf_tensor("BIGC", [128, 8192], F32).ap()
    D1 = nc.alloc_sbuf_tensor("D1", [128, 4096], F32).ap()
    D2 = nc.alloc_sbuf_tensor("D2", [128, 4096], F32).ap()
    D3 = nc.alloc_sbuf_tensor("D3", [128, 4096], F32).ap()
    ident = nc.alloc_sbuf_tensor("ident", [128, 128], F32).ap()
    aexp = nc.alloc_sbuf_tensor("aexp", [128, 128], F32).ap()
    ckt = nc.alloc_sbuf_tensor("ckt", [128, 24], F32).ap()
    dp1 = nc.alloc_sbuf_tensor("dp1", [128, 8], F32).ap()
    dcols = nc.alloc_sbuf_tensor("dcols", [128, 8], F32).ap()
    alogs = nc.alloc_sbuf_tensor("alogs", [128, 128], F32).ap()
    wcats = nc.alloc_sbuf_tensor("wcats", [128, 1024], F32).ap()
    wdtps = nc.alloc_sbuf_tensor("wdtps", [64, 1024], F32).ap()
    bbcs = nc.alloc_sbuf_tensor("bbcs", [128, 1], F32).ap()
    nbbcs = nc.alloc_sbuf_tensor("nbbcs", [16, 1], F32).ap()
    bdtpc = nc.alloc_sbuf_tensor("bdtpc", [128, 8], F32).ap()
    nbdtpc = nc.alloc_sbuf_tensor("nbdtpc", [128, 8], F32).ap()
    cents = nc.alloc_sbuf_tensor("cents", [128, 64], F32).ap()
    lngs = nc.alloc_sbuf_tensor("lngs", [8, 1024], F32).ap()
    lnbs = nc.alloc_sbuf_tensor("lnbs", [8, 1024], F32).ap()
    b1c = nc.alloc_sbuf_tensor("b1c", [128, 256], F32).ap()
    b2c = nc.alloc_sbuf_tensor("b2c", [128, 64], F32).ap()
    ones8 = nc.alloc_sbuf_tensor("ones8", [128, 8], F32).ap()
    iotaEs = nc.alloc_sbuf_tensor("iotaEs", [8, 1], F32).ap()
    toks = nc.alloc_sbuf_tensor("toks", [128, 16], I32).ap()
    slot32 = nc.alloc_sbuf_tensor("slot32", [128, 16], I32).ap()
    slotf = nc.alloc_sbuf_tensor("slotf", [128, 16], F32).ap()
    alphaff = nc.alloc_sbuf_tensor("alphaff", [128, 16], F32).ap()
    rowmaxs = nc.alloc_sbuf_tensor("rowmaxs", [128, 16], F32).ap()
    smallA = nc.alloc_sbuf_tensor("smallA", [128, 96], F32).ap()
    zeros4k = nc.alloc_sbuf_tensor("zeros4k", [128, 1024], F32).ap()
    gapb = nc.alloc_sbuf_tensor("gapb", [128, 256], F32).ap()
    epst = nc.alloc_sbuf_tensor("epst", [128, 1], F32).ap()

    PS = nc.alloc_psum_tensor("PS", [128, 4096], F32).ap()

    def bk(i):
        return PS[:, i * 512:(i + 1) * 512]

    def A4(i):
        return BIGA[:, i * 4096:(i + 1) * 4096]

    def B4(i):
        return BIGB[:, i * 4096:(i + 1) * 4096]

    def C4(i):
        return BIGC[:, i * 4096:(i + 1) * 4096]

    def vgap(cx):
        cx.op("vector", lambda v: v.tensor_copy(out=gapb[:], in_=zeros4k[:, 0:256]), inc=0)

    sem_names = ["bar", "dS", "dA", "dG", "vP", "sP", "tP", "gP"]
    sems = {s: nc.alloc_semaphore(name=s) for s in sem_names}
    cx = Ctx(nc, sems)

    def ph0(cx):
        cx.dma("scalar", ident[:], c_ident[:])
        cx.dma("scalar", ones8[:], c_ones[:])
        cx.dma("scalar", iotaEs[:], c_iotaE[:])
        cx.dma("scalar", zeros4k[:], c_zrow[0:1, 0:1024].partition_broadcast(128))
        cx.dma("scalar", ckt[:], ckl[:])
        cx.dma("scalar", alogs[:].rearrange("p (c n) -> p c n", c=8),
               alog.rearrange("(c p) n -> p c n", p=128))
        cx.dma("scalar", wcats[:], wcatl[:])
        cx.dma("scalar", wdtps[:], wdtp[:, :])
        cx.dma("scalar", bbcs[:], bbcl[:])
        cx.dma("scalar", bdtpc[:], bdtpl[:])
        cx.dma("scalar", dcols[:], dcol_in[:])
        cx.dma("scalar", cents[:], centl[:])
        cx.dma("scalar", lngs[:], lng[:, :])
        cx.dma("scalar", lnbs[:], lnb[:, :])
        cx.dma("scalar", b1c[:], b1l[:])
        cx.dma("scalar", b2c[:], b2l[:])
        cx.dma("scalar", toks[:], tokl[:])
        cx.dma("scalar", epst[:], c_eps[:])
        dA_consts = cx.cnt["dA"]

        cx.wait("gpsimd", "dA", dA_consts)
        for i in range((NSLOT + 128) // 128):
            cx.dma("gpsimd", slotx[i * 128:(i + 1) * 128, :], zeros4k[:, 0:1024])
            cx.dma("gpsimd", sloty[i * 128:(i + 1) * 128, :], zeros4k[:, 0:1024])

        cx.wait("vector", "dA", dA_consts)
        cx.op("vector", lambda v: v.tensor_scalar_mul(out=nbbcs[:], in0=bbcs[0:16, :], scalar1=-1.0))
        cx.op("vector", lambda v: v.tensor_scalar_mul(out=nbdtpc[:], in0=bdtpc[:], scalar1=-1.0))
        cx.wait("scalar", "dA", dA_consts)
        cx.op("scalar", lambda s: s.add(out=dp1[:], in_=dcols[:], add=1.0))
        cx.op("scalar", lambda s: s.activation(out=aexp[:], in_=alogs[:], func=AF.Exp))
        v0 = cx.cnt["vP"]
        s0 = cx.cnt["sP"]

        rowbuf = [A4(0), A4(1), A4(2), A4(3), B4(0), B4(1), C4(0), C4(1)]
        evac_eng = []
        evac_val = []
        tp_vals = []
        for t in range(32):
            tb = D2[:, 0:1024] if t % 2 == 0 else D3[:, 0:1024]
            if t >= 2:
                cx.wait("sync", "tP", tp_vals[t - 2])
            dv = cx.dma("sync", tb, x_in[t * 128:(t + 1) * 128, :])
            for c in range(NCH):
                idx = t * 8 + c
                bnk = idx % 4
                eng = "vector" if idx % 2 == 0 else "scalar"
                cx.wait("tensor", "dS", dv)
                cx.wait("tensor", "dA", dA_consts)
                if idx >= 4:
                    pe = evac_eng[idx - 4]
                    cx.wait("tensor", "vP" if pe == "vector" else "sP", evac_val[idx - 4])
                tv = cx.op("tensor", lambda te, tb=tb, c=c, bnk=bnk: te.transpose(
                    bk(bnk)[:, 0:128], tb[:, c * 128:(c + 1) * 128], ident[:]))
                cx.wait(eng, "tP", tv)
                if eng == "vector":
                    ev = cx.op("vector", lambda v, c=c, t=t, bnk=bnk: v.tensor_copy(
                        out=rowbuf[c][:, t * 128:(t + 1) * 128], in_=bk(bnk)[:, 0:128]))
                else:
                    ev = cx.op("scalar", lambda s, c=c, t=t, bnk=bnk: s.copy(
                        out=rowbuf[c][:, t * 128:(t + 1) * 128], in_=bk(bnk)[:, 0:128]))
                evac_eng.append(eng)
                evac_val.append(ev)
            tp_vals.append(cx.cnt["tP"])
        cx.wait("sync", "vP")
        cx.wait("sync", "sP")
        for c in range(NCH):
            cx.dma("sync", x_fm[c * 128:(c + 1) * 128, :], rowbuf[c])

    def ph1(cx):
        uA, uB = A4(2), A4(3)
        conv_done = []
        for c in range(NCH):
            xb = A4(0) if c % 2 == 0 else A4(1)
            if c >= 2:
                cx.wait("sync", "vP", conv_done[c - 2])
            dv = cx.dma("sync", xb, x_fm[c * 128:(c + 1) * 128, :])
            k0, k1, k2 = ckt[:, c:c + 1], ckt[:, 8 + c:9 + c], ckt[:, 16 + c:17 + c]
            cx.wait("vector", "dS", dv)
            if c >= 1:
                cx.wait("vector", "tP")
            cx.op("vector", lambda v, xb=xb, k0=k0: v.tensor_single_scalar(
                out=uA[:, 1:4096], in_=xb[:, 0:4095], scalar=k0, op=OP.mult))
            cx.op("vector", lambda v, xb=xb, k1=k1: v.scalar_tensor_tensor(
                out=uB[:, 1:4096], in0=xb[:, 1:4096], scalar=k1, in1=uA[:, 1:4096],
                op0=OP.mult, op1=OP.add))
            cx.op("vector", lambda v, xb=xb, k1=k1: v.tensor_single_scalar(
                out=uB[:, 0:1], in_=xb[:, 0:1], scalar=k1, op=OP.mult))
            cx.op("vector", lambda v, xb=xb, k2=k2: v.scalar_tensor_tensor(
                out=uA[:, 0:1], in0=xb[:, 1:2], scalar=k2, in1=uB[:, 0:1],
                op0=OP.mult, op1=OP.add))
            cx.op("vector", lambda v, xb=xb, k2=k2: v.scalar_tensor_tensor(
                out=uA[:, 1:4095], in0=xb[:, 2:4096], scalar=k2, in1=uB[:, 1:4095],
                op0=OP.mult, op1=OP.add))
            vv = cx.op("vector", lambda v: v.tensor_copy(out=uA[:, 4095:4096],
                                                         in_=uB[:, 4095:4096]))
            conv_done.append(vv)
            cx.wait("scalar", "vP", vv)
            sv = cx.op("scalar", lambda s: s.activation(out=uB[:], in_=uA[:], func=AF.Silu))
            if "u_dbg" in _DEV_TAPS:
                cx.wait("sync", "sP", sv)
                du = cx.dma("sync", u_dbg[c * 128:(c + 1) * 128, :], uB[:])
                cx.wait("scalar", "dS", du)
            cx.wait("tensor", "sP", sv)
            for lt in range(NLT):
                cx.op("tensor", lambda te, c=c, lt=lt: te.matmul(
                    PS[0:128, lt * 512:(lt + 1) * 512], wcats[:, c * 128:(c + 1) * 128],
                    uB[:, lt * 512:(lt + 1) * 512],
                    start=(c == 0), stop=(c == NCH - 1)), inc=1 if lt == NLT - 1 else 0)

    def ph2(cx):
        for lt in range(NLT):
            cx.op("scalar", lambda s, lt=lt: s.activation(
                out=D2[0:16, lt * 512:(lt + 1) * 512], in_=PS[0:16, lt * 512:(lt + 1) * 512],
                func=AF.Identity, scale=-1.0, bias=nbbcs[0:16, 0:1]))
            cx.op("scalar", lambda s, lt=lt: s.activation(
                out=D2[32:48, lt * 512:(lt + 1) * 512], in_=PS[32:48, lt * 512:(lt + 1) * 512],
                func=AF.Identity, bias=bbcs[32:48, 0:1]))
            cx.op("scalar", lambda s, lt=lt: s.activation(
                out=D1[0:64, lt * 512:(lt + 1) * 512], in_=PS[64:128, lt * 512:(lt + 1) * 512],
                func=AF.Identity, bias=bbcs[64:128, 0:1]))
        sv0 = cx.cnt["sP"]
        cx.wait("sync", "sP", sv0)
        cx.dma("sync", negBC_d[0:16, :], D2[0:16, :])
        cx.dma("sync", negBC_d[16:32, :], D2[32:48, :])
        sig_val = []
        for c in range(NCH):
            sgb = B4(0) if c % 2 == 0 else B4(1)
            lns = B4(1) if c % 2 == 0 else B4(0)
            if c >= 1:
                cx.wait("scalar", "dS")
            for lt in range(NLT):
                g = c * NLT + lt
                bnk = g % 4
                cx.wait("tensor", "sP", sv0 if g < 4 else sig_val[g - 4])
                tv = cx.op("tensor", lambda te, c=c, lt=lt, bnk=bnk: te.matmul(
                    bk(bnk)[0:128, :], wdtps[:, c * 128:(c + 1) * 128],
                    D1[0:64, lt * 512:(lt + 1) * 512], start=True, stop=True))
                cx.wait("scalar", "tP", tv)
                sg = cx.op("scalar", lambda s, c=c, lt=lt, bnk=bnk, sgb=sgb: s.activation(
                    out=sgb[:, lt * 512:(lt + 1) * 512], in_=bk(bnk),
                    func=AF.Sigmoid, scale=-1.0, bias=nbdtpc[:, c:c + 1]))
                sig_val.append(sg)
            cx.wait("vector", "sP")
            cv = cx.op("vector", lambda v, sgb=sgb: v.tensor_scalar_max(
                out=sgb[:], in0=sgb[:], scalar1=1e-38))
            cx.wait("scalar", "vP", cv)
            s2 = cx.op("scalar", lambda s, sgb=sgb, lns=lns: s.activation(
                out=lns[:], in_=sgb[:], func=AF.Ln))
            cx.wait("sync", "sP", s2)
            cx.dma("sync", lnsig_fm[c * 128:(c + 1) * 128, :], lns)

    def ph3(cx):
        lnsb, xlnsb = A4(0), A4(1)
        alb = [A4(2), A4(3)]
        brep = [B4(0), B4(1)]
        crep = [C4(0), C4(1)]
        bb = [D1, D2]
        scan_vals = []
        ym_vals = []
        br_dma = []
        cr_dma = []
        for c in range(NCH):
            cx.wait("sync", "sP")
            dln = cx.dma("sync", lnsb, lnsig_fm[c * 128:(c + 1) * 128, :])
            cx.wait("sync", "vP")
            dx0 = cx.dma("sync", bb[0][:], x_fm[c * 128:(c + 1) * 128, :])
            cx.wait("vector", "dS", max(dln, dx0))
            cx.op("vector", lambda v: v.tensor_mul(out=xlnsb[:], in0=bb[0][:], in1=lnsb[:]))
            for n in range(N):
                it = c * N + n
                p = it % 2
                if it >= 2:
                    cx.wait("sync", "vP", scan_vals[it - 2])
                dbr = cx.dma("sync", brep[p], negBC_d[n:n + 1, :].partition_broadcast(128))
                br_dma.append(dbr)
                if it >= 2:
                    cx.wait("scalar", "vP", ym_vals[it - 2])
                dcr = cx.dma("scalar", crep[p], negBC_d[N + n:N + n + 1, :].partition_broadcast(128))
                cr_dma.append(dcr)
                cx.wait("scalar", "dS", dln)
                if len(scan_vals) >= 2:
                    cx.wait("scalar", "vP", scan_vals[-2])
                sv = cx.op("scalar", lambda s, c=c, n=n, p=p: s.activation(
                    out=alb[p][:], in_=lnsb[:], func=AF.Exp,
                    scale=aexp[:, c * 16 + n:c * 16 + n + 1]))
                cx.wait("vector", "dS", dbr)
                if it >= 2:
                    cx.wait("vector", "tP", cx.cnt["tP"] - 1)
                cx.op("vector", lambda v, p=p: v.tensor_mul(out=bb[p][:], in0=brep[p][:],
                                                            in1=xlnsb[:]))
                cx.wait("vector", "sP", sv)
                scv = cx.op("vector", lambda v, p=p: v.tensor_tensor_scan(
                    out=D3[:], data0=alb[p][:], data1=bb[p][:], initial=0.0,
                    op0=OP.mult, op1=OP.add))
                scan_vals.append(scv)
                cx.wait("vector", "dA", dcr)
                ymv = cx.op("vector", lambda v, p=p: v.tensor_mul(out=bb[p][:], in0=D3[:],
                                                                  in1=crep[p][:]))
                ym_vals.append(ymv)
                cx.wait("tensor", "vP", ymv)
                for lt in range(NLT):
                    cx.op("tensor", lambda te, lt=lt, n=n, p=p: te.matmul(
                        bk(lt), ident[:], bb[p][:, lt * 512:(lt + 1) * 512],
                        start=(n == 0), stop=(n == N - 1)),
                        inc=1 if lt == NLT - 1 else 0)
            cx.wait("sync", "tP")
            dxr = cx.dma("sync", bb[0][:], x_fm[c * 128:(c + 1) * 128, :])
            cx.wait("vector", "dS", dxr)
            cx.wait("vector", "tP")
            rv = cx.op("vector", lambda v, c=c: v.scalar_tensor_tensor(
                out=xlnsb[:], in0=bb[0][:], scalar=dp1[:, c:c + 1], in1=PS[:, :],
                op0=OP.mult, op1=OP.add))
            cx.wait("sync", "vP", rv)
            cx.dma("sync", resid_fm[c * 128:(c + 1) * 128, :], xlnsb[:])
            cx.wait("tensor", "vP", rv)

    def ph4(cx):
        done = []
        for c in range(NCH):
            rb = A4(0) if c % 2 == 0 else A4(1)
            sq = A4(2) if c % 2 == 0 else A4(3)
            if c >= 2:
                cx.wait("sync", "tP", done[c - 2])
            dv = cx.dma("sync", rb, resid_fm[c * 128:(c + 1) * 128, :])
            cx.wait("scalar", "dS", dv)
            sv = cx.op("scalar", lambda s, rb=rb, sq=sq: s.activation(
                out=sq[:], in_=rb[:], func=AF.Square))
            cx.wait("tensor", "sP", sv)
            for lt in range(NLT):
                cx.op("tensor", lambda te, rb=rb, lt=lt, c=c: te.matmul(
                    PS[0:1, lt * 512:(lt + 1) * 512], ones8[:, 0:1],
                    rb[:, lt * 512:(lt + 1) * 512], start=(c == 0), stop=False), inc=0)
                cx.op("tensor", lambda te, sq=sq, lt=lt, c=c: te.matmul(
                    PS[32:33, lt * 512:(lt + 1) * 512], ones8[:, 0:1],
                    sq[:, lt * 512:(lt + 1) * 512],
                    start=(c == 0), stop=(c == NCH - 1)), inc=1 if lt == NLT - 1 else 0)
            done.append(cx.cnt["tP"])
        cx.wait("vector", "tP")
        cx.op("vector", lambda v: v.tensor_copy(out=B4(0)[0:1, :], in_=PS[0:1, :]))
        vq = cx.op("vector", lambda v: v.tensor_copy(out=D2[32:33, :], in_=PS[32:33, :]))
        cx.wait("sync", "vP", vq)
        dq = cx.dma("sync", B4(1)[0:1, :], D2[32:33, :])
        cx.wait("scalar", "vP", vq)
        cx.wait("scalar", "dS", dq)
        cx.op("scalar", lambda s: s.activation(out=C4(0)[0:1, :], in_=B4(0)[0:1, :],
                                               func=AF.Identity, scale=1.0 / D))
        cx.op("scalar", lambda s: s.activation(out=C4(1)[0:1, :], in_=B4(1)[0:1, :],
                                               func=AF.Identity, scale=1.0 / D))
        cx.op("scalar", lambda s: s.activation(out=B4(0)[0:1, :], in_=C4(0)[0:1, :],
                                               func=AF.Square))
        cx.wait("vector", "sP")
        cx.op("vector", lambda v: v.tensor_sub(out=B4(1)[0:1, :], in0=C4(1)[0:1, :],
                                               in1=B4(0)[0:1, :]))
        cx.wait("scalar", "vP")
        cx.op("scalar", lambda s: s.activation(out=C4(1)[0:1, :], in_=B4(1)[0:1, :],
                                               func=AF.Sqrt, bias=epst[0:1, 0:1]))
        cx.wait("vector", "sP")
        cx.op("vector", lambda v: v.reciprocal(out=D2[0:1, :], in_=C4(1)[0:1, :]))
        vgap(cx)
        cx.op("vector", lambda v: v.tensor_mul(out=B4(0)[0:1, :], in0=C4(0)[0:1, :],
                                               in1=D2[0:1, :]))
        cx.wait("sync", "vP")
        cx.dma("sync", rows_d[0:1, :], B4(0)[0:1, :])
        cx.dma("sync", rows_d[1:2, :], D2[0:1, :])

    def ph5(cx):
        cx.dma("sync", B4(0), rows_d[1:2, :].partition_broadcast(128))
        d1 = cx.dma("sync", B4(1), rows_d[0:1, :].partition_broadcast(128))
        mul_vals = []
        for c in range(NCH):
            rb = A4(0) if c % 2 == 0 else A4(1)
            xo = A4(2) if c % 2 == 0 else A4(3)
            if c >= 2:
                cx.wait("sync", "vP", mul_vals[c - 2] + 1)
            dv = cx.dma("sync", rb, resid_fm[c * 128:(c + 1) * 128, :])
            cx.wait("vector", "dS", dv)
            mv = cx.op("vector", lambda v, rb=rb: v.tensor_mul(out=D1[:], in0=rb[:], in1=B4(0)))
            mul_vals.append(mv)
            vv = cx.op("vector", lambda v, xo=xo: v.tensor_sub(out=xo[:], in0=D1[:], in1=B4(1)))
            cx.wait("sync", "vP", vv)
            cx.dma("sync", xln1_fm[c * 128:(c + 1) * 128, :], xo)

    def ph6(cx):
        ev_hist = []
        tp_hist = []
        wr_hist = []
        for t in range(32):
            tin = D2 if t % 2 == 0 else D3
            tout = BIGB[:, 0:1024] if t % 2 == 0 else BIGB[:, 1024:2048]
            if t >= 2:
                cx.wait("sync", "tP", tp_hist[t - 2])
            src = xln1_fm.rearrange("(c p) l -> p c l", p=128)[:, :, t * 128:(t + 1) * 128]
            dv = cx.dma("sync", tin[:, 0:1024].rearrange("p (c l) -> p c l", c=8), src)
            for c in range(NCH):
                idx = t * 8 + c
                bnk = idx % 2
                cx.wait("tensor", "dS", dv)
                if idx >= 2:
                    cx.wait("tensor", "vP", ev_hist[idx - 2])
                tv = cx.op("tensor", lambda te, tin=tin, c=c, bnk=bnk: te.transpose(
                    bk(bnk)[:, 0:128], tin[:, c * 128:(c + 1) * 128], ident[:]))
                cx.wait("vector", "tP", tv)
                if t >= 2 and c == 0:
                    cx.wait("vector", "dS", wr_hist[t - 2])
                ev = cx.op("vector", lambda v, tout=tout, c=c, bnk=bnk: v.tensor_copy(
                    out=tout[:, c * 128:(c + 1) * 128], in_=bk(bnk)[:, 0:128]))
                ev_hist.append(ev)
            tp_hist.append(cx.cnt["tP"])
            cx.wait("sync", "vP", ev_hist[-1])
            wr = cx.dma("sync", xln1_tm[t * 128:(t + 1) * 128, :], tout)
            wr_hist.append(wr)

    def ph7(cx):
        don = cx.dma("sync", B4(0)[0:8, 0:2048], c_onerow[0:1, 0:2048].partition_broadcast(8))
        for tt in range(NTT):
            g = BIGA[:, tt * 1024:(tt + 1) * 1024]
            dg = cx.idma(g, None, xln1_tm[:, :], IOffs(toks[:, tt:tt + 1], 0))
            for c in range(NCH):
                idx = tt * 8 + c
                cx.wait("tensor", "dG", dg)
                if idx >= 2:
                    cx.wait("tensor", "vP")
                tv = cx.op("tensor", lambda te, g=g, c=c, idx=idx: te.transpose(
                    bk(idx % 2)[:, 0:128], g[:, c * 128:(c + 1) * 128], ident[:]))
                cx.wait("vector", "tP", tv)
                cx.op("vector", lambda v, c=c, idx=idx: v.tensor_copy(
                    out=D1[:, 2048 + c * 128:2048 + (c + 1) * 128], in_=bk(idx % 2)[:, 0:128]))
            cx.wait("tensor", "vP")
            cx.wait("tensor", "sP")
            for c in range(NCH):
                cx.op("tensor", lambda te, c=c: te.matmul(
                    PS[0:8, 1024:1152], cents[:, c * 8:(c + 1) * 8],
                    D1[:, 2048 + c * 128:2048 + (c + 1) * 128],
                    start=(c == 0), stop=(c == NCH - 1)), inc=1 if c == NCH - 1 else 0)
            cx.wait("scalar", "tP")
            se = cx.op("scalar", lambda s: s.copy(out=D2[0:8, 1024:1152], in_=PS[0:8, 1024:1152]))
            cx.wait("tensor", "sP", se)
            cx.wait("tensor", "vP")
            tv2 = cx.op("tensor", lambda te: te.transpose(
                PS[:, 1536:1544], D2[0:8, 1024:1152], ident[0:8, 0:8]))
            cx.wait("vector", "tP", tv2)
            cx.op("vector", lambda v, tt=tt: v.tensor_copy(
                out=D2[:, tt * 8:(tt + 1) * 8], in_=PS[:, 1536:1544]))
            vv = cx.op("vector", lambda v, tt=tt: v.reduce_max(
                out=rowmaxs[:, tt:tt + 1], in_=PS[:, 1536:1544], axis=AX.X))
            cx.wait("scalar", "vP", vv)
            sb = cx.op("scalar", lambda s, tt=tt: s.copy(
                out=smallA[:, 32 + tt % 2:33 + tt % 2], in_=rowmaxs[:, tt:tt + 1]))
            cx.op("scalar", lambda s, tt=tt: s.activation(
                out=alphaff[:, tt:tt + 1], in_=rowmaxs[:, tt:tt + 1], func=AF.Sigmoid), inc=0)
            cx.wait("vector", "sP", sb)
            cx.op("vector", lambda v, tt=tt: v.tensor_single_scalar(
                out=smallA[:, 40:48], in_=D2[:, tt * 8:(tt + 1) * 8],
                scalar=smallA[:, 32 + tt % 2:33 + tt % 2], op=OP.is_equal))
            vgap(cx)
            cx.op("vector", lambda v: v.tensor_tensor_scan(
                out=smallA[:, 48:56], data0=ones8[:], data1=smallA[:, 40:48],
                initial=0.0, op0=OP.mult, op1=OP.add))
            vgap(cx)
            cx.op("vector", lambda v: v.tensor_single_scalar(
                out=smallA[:, 56:64], in_=smallA[:, 48:56], scalar=1.0, op=OP.is_equal))
            vgap(cx)
            vv2 = cx.op("vector", lambda v: v.tensor_mul(
                out=smallA[:, 64:72], in0=smallA[:, 40:48], in1=smallA[:, 56:64]))
            if "oh_dbg" in _DEV_TAPS and tt == 0:
                cx.wait("sync", "vP", vv2)
                cx.dma("sync", oh_dbg[:, :], smallA[:, 0:96])
                cx.dma("sync", aff_dbg[:, 0:8], D2[:, 0:8])
                cx.dma("sync", aff_dbg[:, 8:10], rowmaxs[:, 0:2])
                cx.wait("vector", "dS")
            cx.wait("tensor", "vP", vv2)
            tv3 = cx.op("tensor", lambda te: te.transpose(
                PS[0:8, 2048:2176], smallA[:, 64:72], ident[:]))
            cx.wait("vector", "tP", tv3)
            cx.op("vector", lambda v, tt=tt: v.tensor_copy(
                out=D3[0:8, tt * 128:(tt + 1) * 128], in_=PS[0:8, 2048:2176]))
        cx.wait("vector", "dS", don)
        cx.op("vector", lambda v: v.tensor_tensor_scan(
            out=B4(0)[0:8, 2048:4096], data0=B4(0)[0:8, 0:2048], data1=D3[0:8, 0:2048],
            initial=0.0, op0=OP.mult, op1=OP.add))
        vgap(cx)
        vv3 = cx.op("vector", lambda v: v.tensor_mul(
            out=B4(1)[0:8, 0:2048], in0=D3[0:8, 0:2048], in1=B4(0)[0:8, 2048:4096]))
        cx.wait("tensor", "vP", vv3)
        for q in range(4):
            cx.op("tensor", lambda te, q=q: te.matmul(
                PS[0:1, q * 512:(q + 1) * 512], ones8[0:8, 0:1],
                B4(1)[0:8, q * 512:(q + 1) * 512], start=True, stop=True), inc=0)
            cx.op("tensor", lambda te, q=q: te.matmul(
                PS[0:1, 2048 + q * 512:2048 + (q + 1) * 512], iotaEs[:],
                D3[0:8, q * 512:(q + 1) * 512], start=True, stop=True),
                inc=1 if q == 3 else 0)
        cx.wait("vector", "tP")
        cx.op("vector", lambda v: v.tensor_copy(out=C4(0)[0:1, 0:2048], in_=PS[0:1, 0:2048]))
        cx.op("vector", lambda v: v.tensor_copy(out=C4(0)[0:1, 2048:4096], in_=PS[0:1, 2048:4096]))
        rnk = C4(0)[0:1, 0:2048]
        eidr = C4(0)[0:1, 2048:4096]
        vgap(cx)
        cx.op("vector", lambda v: v.tensor_single_scalar(
            out=C4(1)[0:1, 0:2048], in_=eidr, scalar=float(CAP), op=OP.mult))
        vgap(cx)
        cx.op("vector", lambda v: v.tensor_add(
            out=C4(1)[0:1, 2048:4096], in0=C4(1)[0:1, 0:2048], in1=rnk))
        vgap(cx)
        cx.op("vector", lambda v: v.tensor_single_scalar(
            out=B4(1)[0:1, 2048:4096], in_=C4(1)[0:1, 2048:4096], scalar=1.0, op=OP.subtract))
        cx.op("vector", lambda v: v.tensor_single_scalar(
            out=C4(1)[0:1, 0:2048], in_=rnk, scalar=float(CAP) + 0.5, op=OP.is_ge))
        vgap(cx)
        cx.op("vector", lambda v: v.tensor_single_scalar(
            out=C4(1)[0:1, 2048:4096], in_=B4(1)[0:1, 2048:4096], scalar=float(NSLOT),
            op=OP.subtract))
        vgap(cx)
        cx.op("vector", lambda v: v.tensor_mul(
            out=B4(0)[0:1, 0:2048], in0=C4(1)[0:1, 0:2048], in1=C4(1)[0:1, 2048:4096]))
        vgap(cx)
        vslot = cx.op("vector", lambda v: v.tensor_sub(
            out=C4(0)[0:1, 2048:4096], in0=B4(1)[0:1, 2048:4096], in1=B4(0)[0:1, 0:2048]))
        slotF = C4(0)[0:1, 2048:4096]
        cx.wait("tensor", "vP", vslot)
        for tt in range(NTT):
            cx.wait("tensor", "vP")
            tv4 = cx.op("tensor", lambda te, tt=tt, slotF=slotF: te.transpose(
                PS[:, 2560:2561], slotF[0:1, tt * 128:(tt + 1) * 128], ident[0:1, 0:1]))
            cx.wait("vector", "tP", tv4)
            cx.op("vector", lambda v, tt=tt: v.tensor_copy(
                out=slotf[:, tt:tt + 1], in_=PS[:, 2560:2561]))
        vgap(cx)
        cx.op("vector", lambda v: v.tensor_copy(out=slot32[:], in_=slotf[:]))
        if "aff_dbg" in _DEV_TAPS:
            cx.wait("sync", "vP")
            cx.dma("sync", aff_dbg[:, :], D2[:, 0:128])
        if "route_dbg" in _DEV_TAPS:
            cx.wait("sync", "vP")
            cx.dma("sync", route_dbg[0:1, :], rnk)
            cx.dma("sync", route_dbg[1:2, :], eidr)
            cx.dma("sync", route_dbg[2:3, :], slotF)

    def ph8(cx):
        for tt in range(NTT):
            g = BIGA[:, tt * 1024:(tt + 1) * 1024]
            cx.wait("vector", "dG")
            cx.op("vector", lambda v, g=g: v.reduce_sum(
                out=smallA[:, 0:1], in_=g[:], axis=AX.X))
            cx.wait("scalar", "vP")
            cx.op("scalar", lambda s, g=g: s.activation(
                out=D1[:, 0:1024], in_=g[:], func=AF.Square))
            cx.wait("vector", "sP")
            cx.op("vector", lambda v: v.reduce_sum(
                out=smallA[:, 1:2], in_=D1[:, 0:1024], axis=AX.X))
            cx.wait("scalar", "vP")
            cx.op("scalar", lambda s: s.activation(out=smallA[:, 2:3], in_=smallA[:, 0:1],
                                                   func=AF.Copy, scale=1.0 / D))
            cx.op("scalar", lambda s: s.activation(out=smallA[:, 3:4], in_=smallA[:, 1:2],
                                                   func=AF.Copy, scale=1.0 / D))
            cx.op("scalar", lambda s: s.activation(out=smallA[:, 4:5], in_=smallA[:, 2:3],
                                                   func=AF.Square))
            cx.wait("vector", "sP")
            cx.op("vector", lambda v: v.tensor_sub(out=smallA[:, 5:6], in0=smallA[:, 3:4],
                                                   in1=smallA[:, 4:5]))
            cx.wait("scalar", "vP")
            cx.op("scalar", lambda s: s.activation(out=smallA[:, 6:7], in_=smallA[:, 5:6],
                                                   func=AF.Sqrt, bias=epst[:, 0:1]))
            cx.wait("vector", "sP")
            vv = cx.op("vector", lambda v: v.reciprocal(out=smallA[:, 7:8], in_=smallA[:, 6:7]))
            cx.wait("scalar", "vP", vv)
            cx.op("scalar", lambda s: s.copy(out=smallA[:, 8:9], in_=smallA[:, 2:3]))
            sb = cx.op("scalar", lambda s: s.copy(out=smallA[:, 9:10], in_=smallA[:, 7:8]))
            cx.wait("vector", "sP", sb)
            vg = cx.op("vector", lambda v, g=g: v.tensor_scalar(
                out=D1[:, 1024:2048], in0=g[:], scalar1=smallA[:, 8:9],
                scalar2=smallA[:, 9:10], op0=OP.subtract, op1=OP.mult))
            cx.wait("tensor", "vP", vg)
            for hh in range(2):
                cx.op("tensor", lambda te, tt=tt, hh=hh: te.matmul(
                    bk(hh), D3[0:8, tt * 128:(tt + 1) * 128],
                    lngs[:, hh * 512:(hh + 1) * 512], start=True, stop=True), inc=0)
                cx.op("tensor", lambda te, tt=tt, hh=hh: te.matmul(
                    bk(2 + hh), D3[0:8, tt * 128:(tt + 1) * 128],
                    lnbs[:, hh * 512:(hh + 1) * 512], start=True, stop=True),
                    inc=1 if hh == 1 else 0)
            cx.wait("vector", "tP")
            for hh in range(2):
                cx.op("vector", lambda v, hh=hh: v.tensor_mul(
                    out=D1[:, 2048 + hh * 512:2048 + (hh + 1) * 512],
                    in0=D1[:, 1024 + hh * 512:1024 + (hh + 1) * 512], in1=bk(hh)))
            for hh in range(2):
                cx.op("vector", lambda v, hh=hh: v.tensor_add(
                    out=D1[:, 3072 + hh * 512:3072 + (hh + 1) * 512],
                    in0=D1[:, 2048 + hh * 512:2048 + (hh + 1) * 512], in1=bk(2 + hh)))
            vv2 = cx.cnt["vP"]
            cx.wait("gpsimd", "vP", vv2)
            cx.idma(slotx[:, :], IOffs(slot32[:, tt:tt + 1], 0), D1[:, 3072:4096], None)

    def ph9(cx):
        NST = CAP // 128
        BIGAb = BIGA.bitcast(BF16)
        BIGCb = BIGC.bitcast(BF16)
        D2b = D2.bitcast(BF16)

        def whi(p):
            return BIGCb[:, p * 8192:p * 8192 + 4096]

        def wlo(p):
            return BIGCb[:, p * 8192 + 4096:p * 8192 + 8192]

        h_hi = BIGAb[:, 0:16384]
        h_lo = BIGAb[:, 16384:32768]
        xs_hi = D2b[:, 0:4096]
        xs_lo = D2b[:, 4096:8192]
        wg = []
        for e in range(E):
            cx.wait("gpsimd", "tP")
            cx.wait("gpsimd", "vP")
            dgl = 0
            for st in range(NST):
                dgl = cx.dma("gpsimd", D3[:, st * 1024:(st + 1) * 1024],
                             slotx[e * CAP + st * 128:e * CAP + (st + 1) * 128, :])
            for st in range(NST):
                for c in range(NCH):
                    idx = st * 8 + c
                    cx.wait("tensor", "dG", dgl)
                    if idx >= 2:
                        cx.wait("tensor", "vP")
                    tv = cx.op("tensor", lambda te, st=st, c=c, idx=idx: te.transpose(
                        bk(idx % 2)[:, 0:128],
                        D3[:, st * 1024 + c * 128:st * 1024 + (c + 1) * 128], ident[:]))
                    cx.wait("vector", "tP", tv)
                    cx.op("vector", lambda v, st=st, c=c, idx=idx: v.tensor_copy(
                        out=D1[:, c * CAP + st * 128:c * CAP + (st + 1) * 128],
                        in_=bk(idx % 2)[:, 0:128]))
            cx.op("vector", lambda v: v.tensor_copy(out=xs_hi[:, :], in_=D1[:]))
            vxs = cx.op("vector", lambda v: v.tensor_tensor(
                out=xs_lo[:, :], in0=D1[:], in1=xs_hi[:, :], op=OP.subtract))
            for q in range(8):
                p = q % 2
                if len(wg) >= 2:
                    cx.wait("sync", "tP", wg[-2])
                dw = cx.dma("sync", B4(p).rearrange("p (c f) -> p c f", c=8),
                            w1[e, :, q * 512:(q + 1) * 512].rearrange("(c p) f -> p c f", p=128))
                cx.wait("vector", "dS", dw)
                cx.op("vector", lambda v, p=p: v.tensor_copy(out=whi(p)[:, :], in_=B4(p)))
                wsp = cx.op("vector", lambda v, p=p: v.tensor_tensor(
                    out=wlo(p)[:, :], in0=B4(p), in1=whi(p)[:, :], op=OP.subtract))
                cx.wait("tensor", "vP", max(vxs, wsp))
                for fc in range(4):
                    fi = q * 4 + fc
                    hb = 2 + fi % 2
                    if fi >= 2:
                        cx.wait("tensor", "vP")
                    for c in range(NCH):
                        for ph_, lhs, rhs in ((0, whi(p), xs_hi), (1, whi(p), xs_lo),
                                              (2, wlo(p), xs_hi)):
                            cx.op("tensor", lambda te, lhs=lhs, rhs=rhs, c=c, fc=fc, hb=hb,
                                  ph_=ph_: te.matmul(
                                      bk(hb)[:, 0:CAP], lhs[:, c * 512 + fc * 128:c * 512 + (fc + 1) * 128],
                                      rhs[:, c * CAP:(c + 1) * CAP],
                                      start=(c == 0 and ph_ == 0),
                                      stop=(c == NCH - 1 and ph_ == 2)),
                                  inc=1 if (c == NCH - 1 and ph_ == 2) else 0)
                    cx.wait("scalar", "tP")
                    htmp = D1[:, (fi % 2) * CAP:(fi % 2) * CAP + CAP]
                    if fi >= 2:
                        cx.wait("scalar", "vP")
                    cx.op("scalar", lambda s, e=e, fi=fi, hb=hb, htmp=htmp: s.activation(
                        out=htmp, in_=bk(hb)[:, 0:CAP],
                        func=AF.Relu, bias=b1c[:, e * 32 + fi:e * 32 + fi + 1]))
                    sh = cx.op("scalar", lambda s, fi=fi, htmp=htmp: s.copy(
                        out=h_hi[:, fi * CAP:(fi + 1) * CAP], in_=htmp))
                    cx.wait("vector", "sP", sh)
                    cx.op("vector", lambda v, fi=fi, htmp=htmp: v.tensor_tensor(
                        out=h_lo[:, fi * CAP:(fi + 1) * CAP], in0=htmp,
                        in1=h_hi[:, fi * CAP:(fi + 1) * CAP], op=OP.subtract))
                wg.append(cx.cnt["tP"])
            smm1 = cx.cnt["vP"]
            if "h_dbg" in _DEV_TAPS and e == 0:
                cx.wait("sync", "vP", smm1)
                cx.wait("sync", "sP")
                cx.dma("sync", h_dbg[:, 0:8192], BIGA[:, 0:8192])
                cx.dma("sync", h_dbg[:, 8192:16384], BIGA[:, 8192:16384])
                cx.wait("tensor", "dS")
            for fi in range(32):
                q, fc = fi // 4, fi % 4
                if fc == 0:
                    p = q % 2
                    if len(wg) >= 2:
                        cx.wait("sync", "tP", wg[-2])
                    dw2 = cx.dma("sync", B4(p).rearrange("p (fc d) -> p fc d", fc=4),
                                 w2[e, q * 512:(q + 1) * 512, :]
                                 .rearrange("(fc p) d -> p fc d", p=128))
                    cx.wait("vector", "dS", dw2)
                    cx.op("vector", lambda v, p=p: v.tensor_copy(out=whi(p)[:, :], in_=B4(p)))
                    wsp2 = cx.op("vector", lambda v, p=p: v.tensor_tensor(
                        out=wlo(p)[:, :], in0=B4(p), in1=whi(p)[:, :], op=OP.subtract))
                    cx.wait("tensor", "vP", max(smm1, wsp2))
                for do in range(8):
                    for ph_, lhs, rhsof in ((0, whi(p), 0), (1, whi(p), 16384),
                                            (2, wlo(p), 0)):
                        cx.op("tensor", lambda te, lhs=lhs, fc=fc, do=do, fi=fi, ph_=ph_,
                              rhsof=rhsof: te.matmul(
                                  bk(do)[:, 0:CAP], lhs[:, fc * 1024 + do * 128:fc * 1024 + (do + 1) * 128],
                                  BIGAb[:, rhsof + fi * CAP:rhsof + (fi + 1) * CAP],
                                  start=(fi == 0 and ph_ == 0),
                                  stop=(fi == 31 and ph_ == 2)),
                              inc=1 if (do == 7 and ph_ == 2) else 0)
                if fc == 3:
                    wg.append(cx.cnt["tP"])
            cx.wait("scalar", "tP")
            for do in range(8):
                cx.op("scalar", lambda s, e=e, do=do: s.activation(
                    out=D1[:, do * CAP:(do + 1) * CAP], in_=bk(do)[:, 0:CAP],
                    func=AF.Identity, bias=b2c[:, e * 8 + do:e * 8 + do + 1]))
            sy2 = cx.cnt["sP"]
            for st in range(NST):
                for do in range(8):
                    idx = st * 8 + do
                    cx.wait("tensor", "sP", sy2)
                    if idx >= 2:
                        cx.wait("tensor", "vP")
                    t2 = cx.op("tensor", lambda te, st=st, do=do, idx=idx: te.transpose(
                        bk(idx % 2)[:, 0:128],
                        D1[:, do * CAP + st * 128:do * CAP + (st + 1) * 128], ident[:]))
                    cx.wait("vector", "tP", t2)
                    if e >= 1 and idx == 0:
                        cx.wait("vector", "dG")
                    cx.op("vector", lambda v, st=st, do=do, idx=idx: v.tensor_copy(
                        out=D3[:, st * 1024 + do * 128:st * 1024 + (do + 1) * 128],
                        in_=bk(idx % 2)[:, 0:128]))
            vy2 = cx.cnt["vP"]
            cx.wait("gpsimd", "vP", vy2)
            for st in range(NST):
                cx.dma("gpsimd", sloty[e * CAP + st * 128:e * CAP + (st + 1) * 128, :],
                       D3[:, st * 1024:(st + 1) * 1024])

    def ph10(cx):
        stt_vals = []
        out_wr = []
        for tt in range(NTT):
            fb = D1 if tt % 2 == 0 else D2
            if tt >= 2:
                cx.wait("gpsimd", "vP", stt_vals[tt - 2])
            dg1 = cx.idma(fb[:, 0:1024], None, sloty[:, :], IOffs(slot32[:, tt:tt + 1], 0))
            dg2 = cx.idma(fb[:, 1024:2048], None, xln1_tm[:, :], IOffs(toks[:, tt:tt + 1], 0))
            cx.wait("vector", "dG", dg2)
            cx.op("vector", lambda v, fb=fb, tt=tt: v.tensor_single_scalar(
                out=D3[:, 0:1024], in_=fb[:, 0:1024], scalar=alphaff[:, tt:tt + 1],
                op=OP.mult))
            vgap(cx)
            sv0 = cx.op("vector", lambda v, fb=fb: v.scalar_tensor_tensor(
                out=D3[:, 1024:2048], in0=fb[:, 1024:2048], scalar=2.0, in1=D3[:, 0:1024],
                op0=OP.mult, op1=OP.add))
            stt_vals.append(sv0)
            vgap(cx)
            cx.op("vector", lambda v: v.reduce_sum(out=smallA[:, 16:17],
                                                   in_=D3[:, 1024:2048], axis=AX.X))
            cx.wait("scalar", "vP")
            cx.op("scalar", lambda s: s.activation(out=D3[:, 0:1024], in_=D3[:, 1024:2048],
                                                   func=AF.Square))
            cx.wait("vector", "sP")
            cx.op("vector", lambda v: v.reduce_sum(out=smallA[:, 17:18],
                                                   in_=D3[:, 0:1024], axis=AX.X))
            cx.wait("scalar", "vP")
            cx.op("scalar", lambda s: s.activation(out=smallA[:, 18:19], in_=smallA[:, 16:17],
                                                   func=AF.Copy, scale=1.0 / D))
            cx.op("scalar", lambda s: s.activation(out=smallA[:, 19:20], in_=smallA[:, 17:18],
                                                   func=AF.Copy, scale=1.0 / D))
            cx.op("scalar", lambda s: s.activation(out=smallA[:, 20:21], in_=smallA[:, 18:19],
                                                   func=AF.Square))
            cx.wait("vector", "sP")
            cx.op("vector", lambda v: v.tensor_sub(out=smallA[:, 21:22], in0=smallA[:, 19:20],
                                                   in1=smallA[:, 20:21]))
            cx.wait("scalar", "vP")
            cx.op("scalar", lambda s: s.activation(out=smallA[:, 22:23], in_=smallA[:, 21:22],
                                                   func=AF.Sqrt, bias=epst[:, 0:1]))
            cx.wait("vector", "sP")
            vr = cx.op("vector", lambda v: v.reciprocal(out=smallA[:, 23:24],
                                                        in_=smallA[:, 22:23]))
            cx.wait("scalar", "vP", vr)
            cx.op("scalar", lambda s: s.copy(out=smallA[:, 24:25], in_=smallA[:, 18:19]))
            sb = cx.op("scalar", lambda s: s.copy(out=smallA[:, 25:26], in_=smallA[:, 23:24]))
            cx.wait("vector", "sP", sb)
            ob = D3[:, 2048:3072] if tt % 2 == 0 else D3[:, 3072:4096]
            if tt >= 2:
                cx.wait("vector", "dS", out_wr[tt - 2])
            vv = cx.op("vector", lambda v, ob=ob: v.tensor_scalar(
                out=ob, in0=D3[:, 1024:2048], scalar1=smallA[:, 24:25],
                scalar2=smallA[:, 25:26], op0=OP.subtract, op1=OP.mult))
            cx.wait("sync", "vP", vv)
            wr = cx.dma("sync", out_rows[tt * 128:(tt + 1) * 128, :], ob)
            out_wr.append(wr)

    phases = [ph0, ph1, ph2, ph3, ph4, ph5, ph6, ph7, ph8, ph9, ph10]
    phases = phases[:_LAST_PHASE + 1]

    def engine_program(name):
        def run(eng):
            cx.begin(name, eng)
            for ph in phases:
                ph(cx)
                cx.barrier()
        return run

    with nc.Block() as block:
        block.sync(engine_program("sync"))
        block.scalar(engine_program("scalar"))
        block.vector(engine_program("vector"))
        block.tensor(engine_program("tensor"))
        block.gpsimd(engine_program("gpsimd"))

    return nc


_CACHE = {}


def _get_nc():
    key = (_LAST_PHASE, tuple(_DEV_TAPS))
    if key not in _CACHE:
        _CACHE[key] = _build()
    return _CACHE[key]


def _in_maps(inputs):
    xfull = np.ascontiguousarray(inputs["L0_attn_x"], dtype=np.float32)
    consts = {
        "c_ident": np.eye(128, dtype=np.float32),
        "c_ones": np.ones((128, 8), np.float32),
        "c_iotaE": np.arange(8, dtype=np.float32).reshape(8, 1),
        "c_onerow": np.ones((1, 4096), np.float32),
        "c_zrow": np.zeros((1, 4096), np.float32),
        "c_eps": np.full((128, 1), 1e-5, np.float32),
    }
    wmap = {
        "A_log": "L0_attn_A_log", "W_dtp": "L0_attn_W_dtp",
        "ln_g": "L0_ffn_ln_g", "ln_b": "L0_ffn_ln_b", "w1": "L0_ffn_w1",
        "w2": "L0_ffn_w2",
    }
    weights = {k: np.ascontiguousarray(inputs[v], dtype=np.float32) for k, v in wmap.items()}
    f32c = lambda a: np.ascontiguousarray(a, dtype=np.float32)
    weights["ck_l"] = f32c(np.asarray(inputs["L0_attn_conv_kernel"], np.float32)
                           .reshape(3, 8, 128).transpose(2, 0, 1).reshape(128, 24))
    _wc = np.zeros((1024, 128), np.float32)
    _wbc_np = np.asarray(inputs["L0_attn_W_bc"], np.float32)
    _wc[:, 0:16] = _wbc_np[:, 0:16]
    _wc[:, 32:48] = _wbc_np[:, 16:32]
    _wc[:, 64:128] = np.asarray(inputs["L0_attn_W_dt"], np.float32)
    weights["wcat_l"] = f32c(_wc.reshape(8, 128, 128).transpose(1, 0, 2).reshape(128, 1024))
    _bl = np.zeros((128, 1), np.float32)
    _bl[0:16, 0] = np.asarray(inputs["L0_attn_b_bc"], np.float32)[0:16]
    _bl[32:48, 0] = np.asarray(inputs["L0_attn_b_bc"], np.float32)[16:32]
    _bl[64:128, 0] = np.asarray(inputs["L0_attn_b_dt"], np.float32)
    weights["b_bc_l"] = _bl
    weights["b_dtp_l"] = f32c(np.asarray(inputs["L0_attn_b_dtp"], np.float32).reshape(8, 128).T)
    weights["Dco_l"] = f32c(np.asarray(inputs["L0_attn_Dcoeff"], np.float32).reshape(8, 128).T)
    weights["cent_l"] = f32c(np.asarray(inputs["L0_ffn_centroids"], np.float32)
                             .reshape(8, 8, 128).transpose(2, 1, 0).reshape(128, 64))
    weights["b1_l"] = f32c(np.asarray(inputs["L0_ffn_b1"], np.float32)
                           .reshape(8, 32, 128).transpose(2, 0, 1).reshape(128, 256))
    weights["b2_l"] = f32c(np.asarray(inputs["L0_ffn_b2"], np.float32)
                           .reshape(8, 8, 128).transpose(2, 0, 1).reshape(128, 64))
    maps = []
    for j in range(8):
        b, h = j // 2, j % 2
        m = dict(consts)
        m.update(weights)
        m["x"] = xfull[b]
        m["tok_l"] = np.ascontiguousarray(
            (h * TPC + np.arange(TPC)).astype(np.int32).reshape(16, 128).T)
        maps.append(m)
    return maps


def kernel(**inputs):
    nc = _get_nc()
    res = run_bass_kernel_spmd(nc, _in_maps(inputs), core_ids=list(range(8)))
    out = np.empty((B, L, D), np.float32)
    for j in range(8):
        b, h = j // 2, j % 2
        out[b, h * TPC:(h + 1) * TPC] = res.results[j]["out_rows"]
    return out
